# version 15
# baseline (speedup 1.0000x reference)
import numpy as np
import concourse.bass as bass
import concourse.mybir as mybir
from concourse.bass_utils import run_bass_kernel_spmd

F32 = mybir.dt.float32
OP = mybir.AluOpType
AF = mybir.ActivationFunctionType
AX = mybir.AxisListType.X

H, HD, ROPE, LAT = 16, 64, 32, 128
NEG = -1e30
LN_EPS = 1e-5
RMS_EPS = 1e-6



def _consts():
    c = {}
    c["ident"] = np.eye(128, dtype=np.float32)
    p2 = np.zeros((4, 128, 16), dtype=np.float32)
    for j in range(4):
        for r in range(128):
            p2[j, r, 4 * j + r // 32] = 1.0
    c["p2"] = p2
    c["e0"] = np.ascontiguousarray(p2.transpose(0, 2, 1))
    pn = np.zeros((8, 16, 128), dtype=np.float32)
    for u in range(8):
        for rr in range(128):
            if rr % 64 < 34:
                pn[u, 8 * (rr // 64) + u, rr] = 1.0
    c["patnorm"] = np.ascontiguousarray(pn)
    deni = np.zeros((1, 528), dtype=np.float32)
    for h in range(H):
        deni[0, 33 * h + 32] = 1.0
    c["denind"] = deni
    return c


def _core_inputs(inputs, core, T):
    nslot = T // 256
    b, odd = core // 2, core % 2
    x = np.asarray(inputs["L0_attn_x"], dtype=np.float32)
    Wq = np.asarray(inputs["L0_attn_Wq"], dtype=np.float32)
    Wv = np.asarray(inputs["L0_attn_Wv"], dtype=np.float32)
    Wo = np.asarray(inputs["L0_attn_Wo"], dtype=np.float32)
    d = {}
    xb = np.ascontiguousarray(x[b % x.shape[0]][:T])
    d["xb"] = xb
    own = np.concatenate([np.arange(256 * i + 128 * odd, 256 * i + 128 * odd + 128)
                          for i in range(nslot)])
    d["xown"] = np.ascontiguousarray(xb[own])
    perm = np.concatenate([np.arange(64 * h, 64 * h + 32) for h in range(H)] +
                          [np.arange(64 * h + 32, 64 * h + 64) for h in range(H)])
    d["wq"] = np.ascontiguousarray(Wq[:, perm])
    d["wkv"] = np.ascontiguousarray(np.asarray(inputs["L0_attn_Wkv"], dtype=np.float32))
    d["wk"] = np.ascontiguousarray(np.asarray(inputs["L0_attn_Wk"], dtype=np.float32))
    wva = np.zeros((LAT, 544), dtype=np.float32)
    for h in range(H):
        wva[:, 33 * h:33 * h + 32] = Wv[:, 32 * h:32 * h + 32]
    d["wva"] = wva
    wop = np.zeros((1024, 1024), dtype=np.float32)
    for h in range(H):
        wop[64 * h:64 * h + 32] = Wo[64 * h:64 * h + 32]
    d["wop"] = wop
    d["qg"] = np.asarray(inputs["L0_attn_q_g"], dtype=np.float32).reshape(1, HD).copy()
    d["kg"] = np.asarray(inputs["L0_attn_k_g"], dtype=np.float32).reshape(1, ROPE).copy()
    d["fw"] = np.ascontiguousarray(np.asarray(inputs["L0_ffn_weight"], dtype=np.float32))
    d["fb"] = np.asarray(inputs["L0_ffn_bias"], dtype=np.float32).reshape(1, 1024).copy()
    sr = np.arange(128).reshape(128, 1)
    qr = np.arange(128).reshape(1, 128)
    tri = np.where(sr > qr, np.float32(NEG), np.float32(0)).astype(np.float32)
    allm = np.full((128, 128), np.float32(NEG), dtype=np.float32)
    zero = np.zeros((128, 128), dtype=np.float32)
    d["m0w"] = np.ascontiguousarray(zero if odd else tri)
    d["m1w"] = np.ascontiguousarray(tri if odd else allm)
    NT_, NO_ = T // 128, T // 256
    osel = np.zeros((NT_, NO_), dtype=np.float32)
    for i in range(NO_):
        osel[2 * i + odd, i] = 1.0
    d["ownsel"] = osel
    d.update(_consts())
    return d, own



N_LD, N_GD = 16, 8
DMA_POOL = {"sync": [f"ld{i}" for i in range(N_LD)],
            "gpsimd": [f"gd{i}" for i in range(N_GD)]}
SEM_NAMES = ["pe", "dv", "ac"] + DMA_POOL["sync"] + DMA_POOL["gpsimd"]


class Walk:
    def __init__(self, nc, cur, handle, sems):
        self.nc = nc
        self.cur = cur
        self.h = handle
        self.sems = sems
        self.cnt = {k: 0 for k in sems}
        self.rr = {"sync": 0, "gpsimd": 0}
        self.waited = {k: 0 for k in sems}

    def wait(self, engname, tok):
        if tok is None or self.cur != engname:
            return
        sem, val = tok
        if val > self.waited[sem]:
            self.h.wait_ge(self.sems[sem], val)
            self.waited[sem] = val

    def tok(self, sem):
        return (sem, self.cnt[sem])

    def all_dma(self, q):
        return [(s, self.cnt[s]) for s in DMA_POOL[q] if self.cnt[s] > 0]

    def waitall_sync(self):
        for t in self.all_dma("sync") + self.all_dma("gpsimd"):
            self.wait("sync", t)

    def gate(self, tokens):
        if self.cur in ("tensor", "vector", "scalar", "gpsimd"):
            for t in tokens:
                self.wait(self.cur, t)

    def _emit(self, engname, fn, inc, amt, waits):
        for t in waits:
            self.wait(engname, t)
        if self.cur == engname:
            inst = fn(self.h)
            if inc is not None and inst is not None:
                inst.then_inc(self.sems[inc], amt)
        if inc is not None:
            self.cnt[inc] += amt
            return (inc, self.cnt[inc])
        return None

    def dma(self, q, out, in_, waits=()):
        pool = DMA_POOL[q]
        sem = pool[self.rr[q] % len(pool)]
        self.rr[q] += 1
        prior = self.cnt[sem]
        flow = [(sem, prior)] if prior > 0 else []
        return self._emit(q, lambda e: e.dma_start(out=out, in_=in_), sem, 16,
                          list(waits) + flow)

    def mm(self, out, lhsT, rhs, start, stop, waits=(), tp=None, skipg=False,
           inc=True):
        return self._emit(
            "tensor",
            lambda e: e.matmul(out, lhsT, rhs, start=start, stop=stop,
                               tile_position=tp, skip_group_check=skipg),
            "pe" if inc else None, 1, waits)

    def pet(self, out, in_, ident, waits=()):
        return self._emit("tensor", lambda e: e.transpose(out, in_, ident),
                          "pe", 1, waits)

    def dve(self, fn, waits=()):
        return self._emit("vector", fn, "dv", 1, waits)

    def act(self, fn, waits=()):
        return self._emit("scalar", fn, "ac", 1, waits)



def build_nc(T=4096, stop_after=None):
    nslot = T // 256
    OWN = 128 * nslot
    sup_w = min(1024, T)
    nsup = T // sup_w
    spt = sup_w // 128
    NT = T // 128
    NO = OWN // 128

    nc = bass.Bass("TRN2", target_bir_lowering=False)
    dx = nc.dram_tensor("xb", [T, 1024], F32, kind="ExternalInput").ap()
    dxo = nc.dram_tensor("xown", [OWN, 1024], F32, kind="ExternalInput").ap()
    dwq = nc.dram_tensor("wq", [1024, 1024], F32, kind="ExternalInput").ap()
    dwkv = nc.dram_tensor("wkv", [1024, LAT], F32, kind="ExternalInput").ap()
    dwk = nc.dram_tensor("wk", [LAT, 512], F32, kind="ExternalInput").ap()
    dwva = nc.dram_tensor("wva", [LAT, 544], F32, kind="ExternalInput").ap()
    dwop = nc.dram_tensor("wop", [1024, 1024], F32, kind="ExternalInput").ap()
    dqg = nc.dram_tensor("qg", [1, HD], F32, kind="ExternalInput").ap()
    dkg = nc.dram_tensor("kg", [1, ROPE], F32, kind="ExternalInput").ap()
    dfw = nc.dram_tensor("fw", [1024, 1024], F32, kind="ExternalInput").ap()
    dfb = nc.dram_tensor("fb", [1, 1024], F32, kind="ExternalInput").ap()
    dm0 = nc.dram_tensor("m0w", [128, 128], F32, kind="ExternalInput").ap()
    dm1 = nc.dram_tensor("m1w", [128, 128], F32, kind="ExternalInput").ap()
    dident = nc.dram_tensor("ident", [128, 128], F32, kind="ExternalInput").ap()
    dosel = nc.dram_tensor("ownsel", [NT, NO], F32, kind="ExternalInput").ap()
    dp2 = nc.dram_tensor("p2", [4, 128, 16], F32, kind="ExternalInput").ap()
    de0 = nc.dram_tensor("e0", [4, 16, 128], F32, kind="ExternalInput").ap()
    dpn = nc.dram_tensor("patnorm", [8, 16, 128], F32, kind="ExternalInput").ap()
    ddeni = nc.dram_tensor("denind", [1, 528], F32, kind="ExternalInput").ap()
    dout = nc.dram_tensor("out", [1024, OWN], F32, kind="ExternalOutput").ap()
    dwdq = nc.dram_tensor("wdq", [1024, 1024], F32, kind="Internal").ap()

    from contextlib import ExitStack
    es = ExitStack()
    with es:
        CAP = 51800
        arena = es.enter_context(nc.sbuf_tensor("arena", [128, CAP], F32))
        cur = [0]

        def alloc(n):
            off = cur[0]
            cur[0] += n
            return off

        def tl(off, w, p0=0, p=128):
            return arena[p0:p0 + p, off:off + w]

        def ap3(off, dims, p0=0, np_=128):
            return bass.AP(arena, p0 * CAP + off, [[CAP, np_]] + dims)

        o_kv = alloc(T)
        o_q32 = alloc(4 * OWN)
        o_probs = alloc(2 * 1024)
        o_scr = alloc(1024)
        o_scr2 = alloc(1024)
        o_m0 = alloc(128)
        o_m1 = alloc(128)
        o_pat = alloc(1024)
        o_p2 = alloc(4 * 16)
        o_id = alloc(128)
        o_rows = alloc(4096)
        o_stat = alloc(4 * 32)
        o_sb = alloc(8)
        o_den = alloc(256)
        o_rden = alloc(256)
        o_osel = alloc(NO)
        o_tm = alloc(2 * 1024)
        o_qsc = alloc(3 * 32)
        o_wk = alloc(512)
        o_wva = alloc(544)
        o_wkv = alloc(8 * LAT)
        o_R4 = alloc(16384)
        o_R5 = alloc(8 * 544 + 64)
        o_R6 = alloc(4096)
        o_qs = o_R4 + 12288
        o_qtm = o_R5
        assert cur[0] <= CAP, f"arena {cur[0]} > {CAP}"

        kvf = lambda w_, off=0: tl(o_kv + off, w_, 0, LAT)
        probs = lambda i, off=0, w=1024: tl(o_probs + i * 1024 + off, w)

        def pb(par, hf, off=0, w=1024):
            base = (o_probs + 1024 * hf) if par == 0 else \
                   (o_scr if hf == 0 else o_scr2)
            return tl(base + off, w)
        scr = lambda off=0, w=1024: tl(o_scr + off, w)
        scr2 = lambda off=0, w=1024, p0=0, p=128: tl(o_scr2 + off, w, p0, p)
        m0t = lambda: tl(o_m0, 128)
        m1t = lambda: tl(o_m1, 128)
        e0t = lambda j: tl(o_pat + 128 * j, 128, 32, 16)
        eqt = lambda j: tl(o_pat + 128 * (4 + j), 128, 32, 16)
        pnt = lambda u: tl(o_pat + 128 * u, 128, 0, 16)
        p2t = lambda j: tl(o_p2 + j * 16, 16)
        idt = lambda: tl(o_id, 128)
        rowp = lambda p0, off, w: tl(o_rows + off, w, p0, 1)
        stc = lambda k, c, w=1: tl(o_stat + k * 32 + c, w)
        oselt = lambda: tl(o_osel, NO, 0, NT)
        denS = lambda i: tl(o_den + 128 * i, 128, 0, 16)
        rdenS = lambda i: tl(o_rden + 128 * i, 128, 0, 16)
        early_wo = T >= 4096
        tmt = lambda i: tl(o_tm + i * 1024, 1024)
        qs = lambda i, off=0, w=1024, p0=0, p=128: tl(o_qs + i * 1024 + off, w, p0, p)
        qsc = lambda i: tl(o_qsc + i * 32, 32)
        qtm = lambda: tl(o_qtm, 1024)
        wkt = lambda: tl(o_wk, 512, 0, LAT)
        wvat = lambda: tl(o_wva, 544, 0, LAT)
        wkvt = lambda i: tl(o_wkv + i * LAT, LAT)
        areps = lambda: tl(o_R4 + 8192, 128)
        wqt = lambda i: tl(o_R4 + i * 1024, 1024)
        qtmp = lambda off=0, w=1024: tl(o_R4 + 8192 + 2048 + off, w)
        wdT = lambda i: tl(o_R4 + i * 1024, 1024)
        xdfm2 = lambda p, i, off=0, w=512: tl(
            (o_R6 if p == 0 else o_R4 + 8192) + i * 512 + off, w)
        xet = lambda i: tl(o_probs + 1024 * i, 1024)
        accslot = lambda s, off=0, w=1024: tl(o_R4 + s * 1024 + off, w)
        vaug = lambda tc, off=0, w=544: tl(o_R5 + tc * 544 + off, w)
        xfms = lambda i: tl(o_R6 + i * 128, 128)
        xdfm = lambda i, off=0, w=512: tl(o_R6 + i * 512 + off, w)
        khat = lambda j, off=0, w=None: tl(o_R6 + j * sup_w + off,
                                           sup_w if w is None else w)
        wo_u = lambda u: (tl((o_kv if early_wo else o_R5) + u * 1024, 1024)
                          if u < 4 else tl(o_R6 + (u - 4) * 1024, 1024))
        partial = lambda off=0, w=128: tl(o_tm + off, w)
        outsb = lambda off=0, w=128: tl(o_tm + 1024 + off, w)
        epsln = lambda p0=0, p=128: tl(o_sb + 0, 1, p0, p)
        epsrm = lambda p0=0, p=128: tl(o_sb + 1, 1, p0, p)
        zcol = lambda p0=0, p=128: tl(o_sb + 2, 1, p0, p)
        mrow = lambda: rowp(0, 0, 128)
        arow = lambda: rowp(0, 128, 128)
        csq = lambda off, w: rowp(0, 256 + off, w)
        cskv = lambda: rowp(0, 1280, LAT)
        denir = lambda off=0, w=528: rowp(0, 1408 + off, w)
        ggr = lambda: rowp(0, 1936, 32)
        qgr = lambda: rowp(0, 2000, HD)
        kgr = lambda: rowp(0, 2064, ROPE)
        ggt = lambda: rowp(0, 2128, 128)
        o16 = lambda: rowp(0, 2272, 16)
        fbr = lambda off, w: rowp(0, 2448 + off, w)
        onesr = lambda w=512: rowp(0, 3472, w)

        ps = es.enter_context(nc.psum_tensor("psA", [128, 2048]))
        ps2 = es.enter_context(nc.psum_tensor("psB", [128, 2048]))
        qint = es.enter_context(nc.sbuf_tensor("qint", [128, 1024],
                                               mybir.dt.int32))
        qintf = qint.bitcast(F32)

        def bank(i, off=0, w=512, p0=0, p=128):
            base = ps if i < 4 else ps2
            j = i if i < 4 else i - 4
            return base[p0:p0 + p, 512 * j + off:512 * j + off + w]

        def setap2(r, off=0, w=512, p0=0, p=128):
            return ps[p0:p0 + p, 512 * r + off:512 * r + off + w]

        sems = {k: es.enter_context(nc.semaphore(k)) for k in SEM_NAMES}
        block = es.enter_context(nc.Block())

        def quant(w, o_in, mcol, acol, waits):
            dve = w.dve
            c_am, c_s, c_az = qsc(0), qsc(1), qsc(2)

            if mcol is not None:
                t = dve(lambda e: e.tensor_scalar(qs(0), tl(o_in, 1024), mcol, None,
                                                  OP.subtract), waits=waits)
            else:
                t = dve(lambda e: e.tensor_copy(qs(0), tl(o_in, 1024)), waits=waits)
            xm3 = ap3(o_qs, [[32, 32], [1, 32]])
            t = dve(lambda e: e.tensor_reduce(c_am, xm3, AX, OP.max,
                                              apply_absolute_value=True), waits=[t])
            if acol is not None:
                t = dve(lambda e: e.tensor_scalar(c_am, c_am, acol, None, OP.mult),
                        waits=[t])
            t = dve(lambda e: e.tensor_scalar(c_s, c_am, 1.0 / 7.5, 1e-30,
                                              OP.mult, OP.max), waits=[t])
            t = dve(lambda e: e.reciprocal(c_az, c_s), waits=[t])
            if acol is not None:
                t = dve(lambda e: e.tensor_scalar(c_az, c_az, acol, 8.0,
                                                  OP.mult, OP.mult), waits=[t])
            else:
                t = dve(lambda e: e.tensor_scalar(c_az, c_az, 8.0, None, OP.mult),
                        waits=[t])
            t = dve(lambda e: e.tensor_scalar(c_s, c_s, 0.125, None, OP.mult),
                    waits=[t])
            az3 = ap3(o_qsc + 64, [[1, 32], [0, 32]])
            s83 = ap3(o_qsc + 32, [[1, 32], [0, 32]])
            q13 = ap3(o_qs + 1024, [[32, 32], [1, 32]])
            t = dve(lambda e: e.tensor_tensor(q13, xm3, az3, OP.mult), waits=[t])
            t = dve(lambda e: e.tensor_tensor(qs(2), qs(1), qs(1), OP.mult), waits=[t])
            t = dve(lambda e: e.tensor_scalar(qs(3), qs(2), 256.0, -16.0,
                                              OP.is_ge, OP.mult), waits=[t])
            t = dve(lambda e: e.tensor_scalar(qs(2), qs(2), 1024.0, -8.0,
                                              OP.is_ge, OP.mult), waits=[t])
            t = dve(lambda e: e.scalar_tensor_tensor(qs(3), qs(3), 1.0, qs(2),
                                                     OP.mult, OP.add), waits=[t])
            t = dve(lambda e: e.tensor_scalar(qs(2), qs(3), 0.03125, 1.0,
                                              OP.mult, OP.add), waits=[t])
            t = dve(lambda e: e.tensor_tensor(qs(2), qs(1), qs(2), OP.mult), waits=[t])
            t = dve(lambda e: e.tensor_scalar(qs(2), qs(2), 16.5, None, OP.add),
                    waits=[t])
            t = dve(lambda e: e.tensor_scalar(qintf[:], qs(2), 8388608.0, None,
                                              OP.add), waits=[t])
            t = dve(lambda e: e.tensor_scalar(qs(0), qintf[:], 8388608.0, None,
                                              OP.subtract), waits=[t])
            t = dve(lambda e: e.tensor_tensor(qs(0), qs(2), qs(0), OP.subtract),
                    waits=[t])
            t = dve(lambda e: e.tensor_scalar(qs(2), qs(0), 0.0, None,
                                              OP.is_le), waits=[t])
            t = dve(lambda e: e.scalar_tensor_tensor(qs(2), qs(2), -0.5, qs(0),
                                                     OP.add, OP.add), waits=[t])
            t = dve(lambda e: e.tensor_scalar(qs(0), qs(3), 0.0078125, 0.0625,
                                              OP.mult, OP.add), waits=[t])
            t = dve(lambda e: e.scalar_tensor_tensor(qs(0), qs(0), 1.0, qs(3),
                                                     OP.mult, OP.mult), waits=[t])
            t = dve(lambda e: e.tensor_tensor(qs(0), qs(2), qs(0), OP.mult), waits=[t])
            t = dve(lambda e: e.tensor_tensor(qs(0), qs(0), qs(2), OP.add), waits=[t])
            t = dve(lambda e: e.tensor_tensor(qs(1), qs(1), qs(0), OP.subtract),
                    waits=[t])
            qo3 = ap3(o_qtm, [[32, 32], [1, 32]])
            t = dve(lambda e: e.tensor_tensor(qo3, q13, s83, OP.mult), waits=[t])
            return t

        def stats(w, dsrc, ntiles, tuse):
            dve, dma = w.dve, w.dma
            last = None
            for it in range(ntiles):
                bi = it % 2
                t_l = dma("sync", tmt(bi)[:], dsrc[128 * it:128 * it + 128, :],
                          waits=[tuse[bi]])
                t1 = dve(lambda e, it=it, bi=bi: e.reduce_sum(
                    stc(0, it), tmt(bi)[:], AX), waits=[t_l])
                t2 = dve(lambda e, bi=bi: e.tensor_tensor(qs(3), tmt(bi)[:],
                                                          tmt(bi)[:], OP.mult),
                         waits=[t1])
                last = dve(lambda e, it=it: e.tensor_reduce(
                    stc(1, it), qs(3), AX, OP.add), waits=[t2])
                tuse[bi] = last
            return last

        def stat_finish(w, n, t_in):
            dve, act = w.dve, w.act
            t = dve(lambda e: e.tensor_scalar(stc(2, 0, n), stc(0, 0, n),
                                              1.0 / 1024, None, OP.mult), waits=[t_in])
            t = dve(lambda e: e.tensor_tensor(qs(3, 0, n), stc(2, 0, n),
                                              stc(2, 0, n), OP.mult), waits=[t])
            t = dve(lambda e: e.scalar_tensor_tensor(qs(3, 64, n), stc(1, 0, n),
                                                     1.0 / 1024, qs(3, 0, n),
                                                     OP.mult, OP.subtract), waits=[t])
            t = act(lambda e: e.activation(qs(3, 0, n), qs(3, 64, n), AF.Ln,
                                           bias=epsln(), scale=1.0), waits=[t])
            t = act(lambda e: e.activation(stc(3, 0, n), qs(3, 0, n), AF.Exp,
                                           bias=zcol(), scale=-0.5), waits=[t])
            t = dve(lambda e: e.tensor_scalar(stc(0, 0, n), stc(2, 0, n), -1.0, None,
                                              OP.mult), waits=[t])
            return t

        def emit(w: Walk):
            dve, act, mm, pet, dma = w.dve, w.act, w.mm, w.pet, w.dma

            tld = []
            tld.append(dma("sync", idt(), dident[:]))
            tld.append(dma("sync", m0t(), dm0[:]))
            tld.append(dma("sync", m1t(), dm1[:]))
            for j in range(4):
                tld.append(dma("sync", p2t(j), dp2[j]))
                tld.append(dma("sync", e0t(j), de0[j]))
            for u in range(8):
                tld.append(dma("sync", pnt(u), dpn[u]))
            t_osel = dma("sync", oselt(), dosel[:])
            tld.append(t_osel)
            tld.append(dma("sync", qgr(), dqg[:]))
            tld.append(dma("sync", kgr(), dkg[:]))
            tld.append(dma("sync", fbr(0, 1024), dfb[:]))
            tld.append(dma("sync", denir(), ddeni[:]))
            t_wkv = []
            for i in range(8):
                t_wkv.append(dma("sync", wkvt(i), dwkv[128 * i:128 * i + 128, :]))
            t_wk = dma("sync", wkt(), dwk[:])
            t_wva = dma("sync", wvat(), dwva[:])
            w.gate(tld + t_wkv + [t_wk, t_wva])

            t_eps = dve(lambda e: e.memset(epsln(), LN_EPS))
            dve(lambda e: e.tensor_copy(
                scr2(512, 1).bitcast(mybir.dt.bfloat16)[0:1, 0:2],
                epsln().bitcast(mybir.dt.bfloat16)[0:1, 0:2]),
                waits=[t_eps])
            dve(lambda e: e.memset(epsrm(), RMS_EPS))
            t_z = dve(lambda e: e.memset(zcol(), 0.0))
            t_or = dve(lambda e: e.memset(onesr(512), 1.0))
            t_oc = dve(lambda e: e.memset(scr(0, 1), 1.0))
            t_o16 = dve(lambda e: e.memset(o16(), 1.0))
            t = dve(lambda e: e.tensor_tensor(ggr(), rowp(0, 2000, 32), kgr(),
                                              OP.mult))
            t = dve(lambda e: e.tensor_scalar(ggr(), ggr(), 0.125, None, OP.mult),
                    waits=[t])
            for rr in range(4):
                t = dve(lambda e, rr=rr: e.tensor_copy(rowp(0, 2128 + 32 * rr, 32),
                                                       ggr()), waits=[t])
            tmm = mm(bank(7, 0, 128, 32, 16), o16(), ggt(), True, True,
                     waits=[t, t_o16])
            t_eq = None
            for j in range(4):
                t_eq = dve(lambda e, j=j: e.tensor_tensor(
                    eqt(j), e0t(j), bank(7, 0, 128, 32, 16), OP.mult), waits=[tmm])

            for i in range(8):
                tmm = mm(bank(6, 0, LAT, 0, 1), scr(0, 1), wkvt(i),
                         i == 0, i == 7, waits=[t_oc, t_wkv[i]], inc=i == 7)
            t_cskv = dve(lambda e: e.tensor_copy(cskv(), bank(6, 0, LAT, 0, 1)),
                         waits=[tmm])

            tuse = [None, None]
            t_st = stats(w, dx, NT, tuse)
            t_fin = stat_finish(w, NT, t_st)

            tuse = [t_fin, t_fin]
            t_kv_prev = None
            for it in range(NT):
                bi = it % 2
                t_l = dma("sync", tmt(bi)[:], dx[128 * it:128 * it + 128, :],
                          waits=[tuse[bi]])
                tpm = pet(bank(3, 0, 128, 0, 1), stc(0, it), idt(),
                          waits=[t_fin, w.tok("dv")])
                t_evm = dve(lambda e: e.tensor_copy(mrow(), bank(3, 0, 128, 0, 1)),
                            waits=[tpm])
                tpa = pet(bank(6, 0, 128, 0, 1), stc(3, it), idt(),
                          waits=[t_fin, w.tok("dv")])
                t_eva = dve(lambda e: e.tensor_copy(arow(), bank(6, 0, 128, 0, 1)),
                            waits=[tpa])
                tra = mm(bank(7, 0, 128), onesr(128), arow(), True, True,
                         waits=[t_eva, t_or])
                t_ar = dve(lambda e: e.tensor_copy(areps(), bank(7, 0, 128)),
                           waits=[tra])
                t_ev = []
                for ci in range(8):
                    tp = pet(bank(ci // 4, 128 * (ci % 4), 128),
                             tmt(bi)[:, 128 * ci:128 * ci + 128], idt(),
                             waits=[t_l, w.tok("dv")])
                    t_ev.append(dve(lambda e, ci=ci: e.tensor_copy(
                        xfms(ci), bank(ci // 4, 128 * (ci % 4), 128)), waits=[tp]))
                for ci in range(8):
                    tmm = mm(bank(2, 0, 128), wkvt(ci), xfms(ci), ci == 0, False,
                             waits=[t_ev[ci]] + ([t_kv_prev] if ci == 0 else []),
                             inc=False)
                tmm = mm(bank(2, 0, 128), cskv(), mrow(), False, True,
                         waits=[t_evm, t_cskv])
                t_kv = dve(lambda e, it=it: e.tensor_tensor(
                    kvf(128, 128 * it), bank(2, 0, 128),
                    areps()[0:LAT, :], OP.mult), waits=[tmm, t_ar])
                t_kv_prev = t_kv
                tuse[bi] = t_kv
            t_kv_done = w.tok("dv")

            if stop_after == "A":
                tA = dma("sync", dout[0:LAT, 0:min(OWN, T)], kvf(min(OWN, T), 0),
                         waits=[t_kv_done])
                w.waitall_sync()
                return
            w.gate(w.all_dma("sync") + w.all_dma("gpsimd"))
            t_wq = []
            for i in range(8):
                t_wq.append(dma("sync", wqt(i), dwq[128 * i:128 * i + 128, :],
                                waits=[t_kv_done]))
            tuse = [t_kv_done, t_kv_done]
            tg1 = pet(bank(4, 0, 128, 0, NT), stc(2, 0, NT), idt(),
                      waits=[t_fin, w.tok("dv")])
            tg1e = dve(lambda e: e.tensor_copy(tl(o_scr2, 128, 0, NT),
                                               bank(4, 0, 128, 0, NT)),
                       waits=[tg1])
            tg2 = pet(bank(5, 0, 128, 0, NT), stc(3, 0, NT), idt(),
                      waits=[t_fin, w.tok("dv")])
            tg2e = dve(lambda e: e.tensor_copy(tl(o_scr2 + 128, 128, 0, NT),
                                               bank(5, 0, 128, 0, NT)),
                       waits=[tg2])
            g1 = mm(bank(4, 128, 128, 0, NO), oselt(), tl(o_scr2, 128, 0, NT),
                    True, True, waits=[tg1e, t_osel])
            g2 = mm(bank(5, 128, 128, 0, NO), oselt(),
                    tl(o_scr2 + 128, 128, 0, NT), True, True,
                    waits=[tg2e, t_osel])
            e1 = dve(lambda e: e.tensor_copy(tl(o_scr2, 128, 0, NO),
                                             bank(4, 128, 128, 0, NO)),
                     waits=[g1])
            e2 = dve(lambda e: e.tensor_copy(tl(o_scr2 + 128, 128, 0, NO),
                                             bank(5, 128, 128, 0, NO)),
                     waits=[g2])
            b1 = pet(bank(4, 256, NO), tl(o_scr2, 128, 0, NO),
                     idt()[0:NO, 0:NO], waits=[e1, w.tok("dv")])
            b2 = pet(bank(5, 256, NO), tl(o_scr2 + 128, 128, 0, NO),
                     idt()[0:NO, 0:NO], waits=[e2, w.tok("dv")])
            f1 = dve(lambda e: e.tensor_copy(stc(2, 0, NO), bank(4, 256, NO)),
                     waits=[b1])
            f2 = dve(lambda e: e.tensor_copy(stc(3, 0, NO), bank(5, 256, NO)),
                     waits=[b2])
            t_fin = dve(lambda e: e.tensor_scalar(stc(0, 0, NO), stc(2, 0, NO),
                                                  -1.0, None, OP.mult),
                        waits=[f1, f2])
            t_csq = None
            for ch in range(2):
                for i in range(8):
                    tmm = mm(bank(7, 0, 512, 0, 1), scr(0, 1),
                             wqt(i)[:, 512 * ch:512 * ch + 512],
                             i == 0, i == 7, waits=[t_wq[i], t_oc, w.tok("dv")],
                             inc=i == 7)
                t_csq = dve(lambda e, ch=ch: e.tensor_copy(
                    csq(512 * ch, 512), bank(7, 0, 512, 0, 1)), waits=[tmm])
            t_pe0 = w.tok("pe")
            wdq_st = [None] * 8
            t_wdq_use = [None, None]

            def wd_quant(wt):
                bi2 = wt % 2
                t_lw = dma("sync", xet(bi2)[:],
                           dfw[128 * wt:128 * wt + 128, :],
                           waits=[t_pe0, t_kv_done] +
                                 ([t_wdq_use[bi2]] if t_wdq_use[bi2] else []))
                t_qz = quant(w, o_probs + 1024 * bi2, None, None,
                             [t_lw] + ([wdq_st[wt - 1]] if wt else []))
                wdq_st[wt] = dma("sync", dwdq[128 * wt:128 * wt + 128, :],
                                 qtm()[:], waits=[t_qz])
                t_wdq_use[bi2] = t_qz

            t_prev_q = [None] * 8
            t_prev_rep = None
            t_prev_lnq = None
            t_32 = None
            for ot in range(NO):
                bi = ot % 2
                t_l = dma("sync", tmt(bi)[:], dxo[128 * ot:128 * ot + 128, :],
                          waits=[tuse[bi]])
                tpm = pet(bank(6, 384, 128, 0, 1), stc(0, ot), idt(),
                          waits=[t_fin, w.tok("dv")])
                t_evm = dve(lambda e: e.tensor_copy(mrow(), bank(6, 384, 128, 0, 1)),
                            waits=[tpm])
                tpa = pet(bank(7, 384, 128, 0, 1), stc(3, ot), idt(),
                          waits=[t_fin, w.tok("dv")])
                t_eva = dve(lambda e: e.tensor_copy(arow(), bank(7, 384, 128, 0, 1)),
                            waits=[tpa])
                tra = mm(bank(7, 128, 128), onesr(128), arow(), True, True,
                         waits=[t_eva, t_or])
                t_ar = dve(lambda e: e.tensor_copy(areps(), bank(7, 128, 128)),
                           waits=[tra])
                t_ev = []
                for ci in range(8):
                    tp = pet(bank(ci // 4, 128 * (ci % 4), 128),
                             tmt(bi)[:, 128 * ci:128 * ci + 128], idt(),
                             waits=[t_l, w.tok("dv")])
                    t_ev.append(dve(lambda e, ci=ci: e.tensor_copy(
                        xfms(ci), bank(ci // 4, 128 * (ci % 4), 128)), waits=[tp]))
                t_q = []
                for j in range(8):
                    bk, off = 2 + j // 4, 128 * (j % 4)
                    for ci in range(8):
                        tmm = mm(bank(bk, off, 128), wqt(ci)[:, 128 * j:128 * j + 128],
                                 xfms(ci), ci == 0 and j % 4 == 0, False,
                                 waits=[t_ev[ci], t_wq[ci]] +
                                       ([t_prev_q[j]] if ci == 0 else []),
                                 inc=False)
                    tmm = mm(bank(bk, off, 128), csq(128 * j, 128), mrow(),
                             False, j % 4 == 3,
                             waits=[t_evm, t_csq], inc=j % 4 == 3)
                    t_q.append(tmm)
                t_qe = []
                for j in range(8):
                    bk, off = 2 + j // 4, 128 * (j % 4)
                    t_qe.append(dve(lambda e, j=j, bk=bk, off=off: e.tensor_tensor(
                        qtmp(128 * j, 128), bank(bk, off, 128),
                        areps(), OP.mult), waits=[t_q[3 + 4 * (j // 4)], t_ar]))
                    t_prev_q[j] = t_qe[-1]
                t_q2 = dve(lambda e: e.tensor_tensor(scr(0, 1024), qtmp(0, 1024),
                                                     qtmp(0, 1024), OP.mult),
                           waits=[t_qe[-1]])
                for j in range(8):
                    tmm = mm(bank(6, 0, 128, 32, 16), p2t(j % 4),
                             scr(128 * j, 128), j == 0, j == 7,
                             waits=[t_q2] + ([t_prev_lnq] if j == 0 and t_prev_lnq
                                             else []), inc=j == 7)
                t_r = act(lambda e: e.activation(scr2(0, 128, 32, 16),
                                                 bank(6, 0, 128, 32, 16), AF.Ln,
                                                 bias=epsrm(32, 16), scale=1.0 / HD),
                          waits=[tmm, t_eps])
                t_r = act(lambda e: e.activation(scr2(128, 128, 32, 16),
                                                 scr2(0, 128, 32, 16), AF.Exp,
                                                 bias=zcol(32, 16), scale=-0.5),
                          waits=[t_r, t_z])
                t_prev_lnq = t_r
                for j in range(4):
                    tmm = mm(bank(7, 0, 128), eqt(j), scr2(128, 128, 32, 16),
                             True, True,
                             waits=[t_r, t_eq] + ([t_prev_rep] if t_prev_rep else []))
                    t_rp = dve(lambda e: e.tensor_copy(scr2(256, 128), bank(7, 0, 128)),
                               waits=[tmm])
                    t_32 = dve(lambda e, j=j, ot=ot: e.tensor_tensor(
                        tl(o_q32 + j * OWN + 128 * ot, 128), qtmp(128 * j, 128),
                        scr2(256, 128), OP.mult), waits=[t_rp])
                    t_prev_rep = t_32
                tuse[bi] = t_32
                for wt in range(8):
                    if wt * NO // 8 == ot:
                        wd_quant(wt)
            t_q32_done = w.tok("dv")

            if stop_after == "B":
                for j in range(4):
                    dma("sync", dout[128 * j:128 * j + 128, 0:OWN],
                        tl(o_q32 + j * OWN, OWN), waits=[t_q32_done])
                w.waitall_sync()
                return
            w.gate(w.all_dma("sync") + w.all_dma("gpsimd"))
            tuse = [t_q32_done, t_q32_done]
            t_wdT = None
            for wt in range(8):
                bi = wt % 2
                t_l = dma("sync", tmt(bi)[:], dwdq[128 * wt:128 * wt + 128, :],
                          waits=[tuse[bi], wdq_st[wt], t_wq[wt]])
                tp_last = None
                for ci in range(8):
                    tp = pet(bank(ci // 4, 128 * (ci % 4), 128),
                             tmt(bi)[:, 128 * ci:128 * ci + 128], idt(),
                             waits=[t_l, w.tok("dv")])
                    tp_last = tp
                    t_wdT = dve(lambda e, ci=ci, wt=wt: e.tensor_copy(
                        wdT(ci)[:, 128 * wt:128 * wt + 128],
                        bank(ci // 4, 128 * (ci % 4), 128)), waits=[tp])
                tuse[bi] = tp_last
            if stop_after == "C0":
                for ci in range(4):
                    dma("sync", dout[128 * ci:128 * ci + 128, 0:min(OWN, 1024)],
                        wdT(ci)[:, 0:min(OWN, 1024)], waits=[w.tok("dv")])
                w.waitall_sync()
                return
            t_pw = []
            t_prev_f = [None] * 4
            n_oc = (OWN + 511) // 512
            t_sub_oc = [None] * n_oc
            bias_mm = {}

            def ffn_half(oc_p, jlist):
                po = oc_p % 2
                cw_p = min(512, OWN - 512 * oc_p)
                for j in jlist:
                    bk = 4 + j % 4
                    for ci in range(8):
                        mm(bank(bk, 0, cw_p),
                           wdT(ci)[:, 128 * j:128 * j + 128],
                           xdfm2(po, ci, 0, cw_p), ci == 0, False,
                           waits=([t_sub_oc[oc_p], t_wdT] if ci == 0
                                  else []) +
                                 ([t_prev_f[j % 4]] if ci == 0 and
                                  t_prev_f[j % 4] else []), inc=False)
                    bias_mm[(oc_p, j)] = mm(
                        bank(bk, 0, cw_p), fbr(128 * j, 128), onesr(cw_p),
                        False, True, waits=[t_or])

            def tf_half(oc_p, jlist):
                cw_p = min(512, OWN - 512 * oc_p)
                for j in jlist:
                    bk = 4 + j % 4
                    hb = 512 * (j % 2)
                    t_f = dve(lambda e, j=j, bk=bk, hb=hb, cw_p=cw_p:
                              e.tensor_copy(scr2(hb, cw_p), bank(bk, 0, cw_p)),
                              waits=[bias_mm[(oc_p, j)]] +
                                    ([t_pw[-2]] if len(t_pw) >= 2 else []))
                    t_prev_f[j % 4] = t_f
                    t_pw.append(dma("sync",
                                    dout[128 * j:128 * j + 128,
                                         512 * oc_p:512 * oc_p + cw_p],
                                    scr2(hb, cw_p), waits=[t_f]))

            hold = {"t_sub": None}

            def qtile(ot, par):
                bi = ot % 2
                t_l = dma("sync", tmt(bi)[:], dxo[128 * ot:128 * ot + 128, :],
                          waits=[tuse[bi]])
                t_qz = quant(w, o_tm + bi * 1024,
                             None if stop_after == 'Cq2' else stc(2, ot),
                             None if stop_after == 'Cq2' else stc(3, ot),
                             [t_l])
                if stop_after in ("Cq", "Cq2") and ot == 0:
                    dma("sync", dout[0:128, 0:min(OWN, 1024)],
                        qtm()[:, 0:min(OWN, 1024)], waits=[t_qz])
                    w.waitall_sync()
                    return True
                sub = ot % 4
                t_sub = hold["t_sub"]
                for ci in range(8):
                    tp = pet(bank(ci // 4, 128 * (ci % 4), 128),
                             qtm()[:, 128 * ci:128 * ci + 128], idt(),
                             waits=[t_qz, w.tok("dv")])
                    t_sub = dve(lambda e, ci=ci, sub=sub, par=par:
                                e.tensor_copy(
                        xdfm2(par, ci, 128 * sub, 128),
                        bank(ci // 4, 128 * (ci % 4), 128)), waits=[tp])
                if stop_after == "Cx" and ot == 0:
                    dma("sync", dout[0:128, 0:min(OWN, 512)],
                        xdfm2(0, 0, 0, min(OWN, 512)), waits=[w.tok("dv")])
                    w.waitall_sync()
                    return True
                tuse[bi] = t_qz
                hold["t_sub"] = t_sub
                return False

            for oc in range(n_oc + 1):
                subs = (list(range(min(512, OWN - 512 * oc) // 128))
                        if oc < n_oc else [])
                hh = (len(subs) + 1) // 2
                par = oc % 2
                if oc >= 1:
                    ffn_half(oc - 1, [0, 1, 2, 3])
                for sub in subs[:hh]:
                    if qtile(4 * oc + sub, par):
                        return
                if oc >= 1:
                    tf_half(oc - 1, [0, 1, 2, 3])
                    ffn_half(oc - 1, [4, 5, 6, 7])
                for sub in subs[hh:]:
                    if qtile(4 * oc + sub, par):
                        return
                if oc < n_oc:
                    t_sub_oc[oc] = hold["t_sub"]
                    if stop_after == "C1" and oc == 0:
                        cw1 = min(OWN, 512)
                        for ci in range(8):
                            dma("sync", dout[128 * ci:128 * ci + 128, 0:cw1],
                                xdfm2(0, ci, 0, cw1), waits=[w.tok("dv")])
                        w.waitall_sync()
                        return
                if oc >= 1:
                    tf_half(oc - 1, [4, 5, 6, 7])
            t_pw_done = None

            if stop_after == "C":
                w.waitall_sync()
                for t in w.all_dma("gpsimd"):
                    w.wait("gpsimd", t)
                return
            w.gate(w.all_dma("sync") + w.all_dma("gpsimd"))
            t_vp = dve(lambda e: e.memset(tl(o_R5 + spt * 544, 64), 0.0))
            for tc in range(spt):
                t_vp = dve(lambda e, tc=tc: e.memset(vaug(tc, 528, 16), 0.0),
                           waits=[t_vp])
            d0 = w.tok("dv")
            exp_tok = [d0, d0]
            av_clear = d0
            probs_read = {(p_, h_): d0 for p_ in range(2) for h_ in range(2)}
            pcnt = [0]
            acc_init = [False] * nslot
            t_acc_ev = {}
            t_khd = None
            t_vad = None
            t_prev_lnk = None

            for sup in range(nsup):
                nch = (sup_w + 511) // 512
                t_e_list = []
                for j in range(4):
                    for ch in range(nch):
                        cw = min(512, sup_w - 512 * ch)
                        tmm = mm(bank(ch, 0, cw), wkt()[:, 128 * j:128 * j + 128],
                                 kvf(cw, sup * sup_w + 512 * ch), True, True,
                                 waits=[t_kv_done, exp_tok[0], exp_tok[1],
                                        av_clear, w.tok("dv")], inc=ch == nch - 1)
                    t_e = dve(lambda e, j=j: e.tensor_copy(
                        khat(j), ps[:, 0:sup_w]), waits=[tmm])
                    t_e_list.append(t_e)
                    t_k2 = dve(lambda e, j=j: e.tensor_tensor(
                        scr(0, sup_w), khat(j), khat(j), OP.mult),
                        waits=[t_e] + ([probs_read[(1, 0)]] if j == 0 else []))
                    for ch in range(nch):
                        cw = min(512, sup_w - 512 * ch)
                        tmm = mm(bank(2 + ch, 0, cw, 32, 16), p2t(j),
                                 scr(512 * ch, cw), j == 0, j == 3,
                                 waits=[t_k2] + ([t_prev_lnk, exp_tok[1]]
                                                 if j == 0 else []),
                                 inc=j == 3 and ch == nch - 1)
                t_r = act(lambda e: e.activation(scr2(0, sup_w, 32, 16),
                                                 ps[32:48, 1024:1024 + sup_w], AF.Ln,
                                                 bias=epsrm(32, 16), scale=1.0 / ROPE),
                          waits=[tmm, t_eps, probs_read[(1, 1)]])
                t_r = act(lambda e: e.activation(probs(0, 0, sup_w)[32:48, :],
                                                 scr2(0, sup_w, 32, 16), AF.Exp,
                                                 bias=zcol(32, 16), scale=-0.5),
                          waits=[t_r, t_z, probs_read[(0, 0)]])
                t_prev_lnk = t_r
                t_kh = None
                for j in range(4):
                    for ch in range(nch):
                        cw = min(512, sup_w - 512 * ch)
                        tmm = mm(bank(ch, 0, cw), e0t(j),
                                 probs(0, 512 * ch, cw)[32:48, :], True, True,
                                 waits=[t_r, w.tok("dv")], inc=ch == nch - 1)
                    probs_read[(0, 0)] = tmm
                    t_rp = dve(lambda e: e.tensor_copy(scr(0, sup_w), ps[:, 0:sup_w]),
                               waits=[tmm])
                    t_kh = dve(lambda e, j=j: e.tensor_tensor(
                        khat(j), khat(j), scr(0, sup_w), OP.mult), waits=[t_rp])
                t_khd = t_kh
                if stop_after == "D0" and sup == 0:
                    dma("sync", dout[0:128, 0:min(OWN, sup_w)],
                        khat(0, 0, min(OWN, sup_w)), waits=[t_khd])
                    w.waitall_sync()
                    for t in w.all_dma("gpsimd"):
                        w.wait("gpsimd", t)
                    return
                t_v = None
                for tc in range(spt):
                    base = sup * sup_w + 128 * tc
                    for (c0, cw, bk) in [(0, 512, 0), (512, 16, 1)]:
                        tmm = mm(bank(bk, 0, cw), kvf(128, base),
                                 wvat()[:, c0:c0 + cw], True, False,
                                 waits=[t_khd, t_wva, w.tok("dv")], inc=False)
                        tmm = mm(bank(bk, 0, cw), onesr(128),
                                 denir(c0, cw), False, True, waits=[t_or])
                        t_v = dve(lambda e, tc=tc, c0=c0, cw=cw, bk=bk: e.tensor_copy(
                            vaug(tc, c0, cw), bank(bk, 0, cw)), waits=[tmm])
                t_vad = t_v
                if early_wo and sup == nsup - 1:
                    t_kv_free = w.tok("pe")
                    t_wo_early = []
                    for u in range(4):
                        t_wo_early.append(dma(
                            "sync", wo_u(u), dwop[128 * u:128 * u + 128, :],
                            waits=[t_kv_free]))
                if stop_after == "D1" and sup == 0:
                    dma("sync", dout[0:128, 0:min(OWN, 544)],
                        vaug(0, 0, min(OWN, 544)), waits=[t_vad])
                    w.waitall_sync()
                    for t in w.all_dma("gpsimd"):
                        w.wait("gpsimd", t)
                    return

                for slot in range(nslot):
                    lo = sup * spt
                    hi = min(2 * slot + 2, sup * spt + spt)
                    if lo >= hi:
                        continue
                    av_started = {}
                    av_last = {}
                    av_order = ([h for h in range(16) if h % 4 < 2] +
                                [h for h in range(16) if h % 4 >= 2])
                    for st in range(lo, hi):
                        for head in av_order:
                            av_last[((head // 2) // 4, 64 * (head % 2))] = (st, head)
                    t_avmm = None
                    for st in range(lo, hi):
                        masked = st in (2 * slot, 2 * slot + 1)
                        par = pcnt[0] % 2
                        pcnt[0] += 1
                        t_exp = [None, None]
                        lastb = [None] * 4
                        for r in range(4):
                          for j in range(4):
                            h = 4 * j + r
                            lastb[r] = mm(
                                setap2(r, 128 * j, 128),
                                khat(h // 4, 128 * (st - sup * spt), 128)[32 * r:32 * r + 32, :],
                                tl(o_q32 + (h // 4) * OWN + 128 * slot, 128)[32 * r:32 * r + 32, :],
                                j == 0, j == 3 and not masked,
                                waits=[t_vad, t_q32_done, exp_tok[r // 2]],
                                tp=(32 * r, 0), inc=j == 3 and not masked)
                        if masked:
                            mw = m0t if st == 2 * slot else m1t
                            for r in range(4):
                                for j in range(4):
                                    lastb[r] = mm(setap2(r, 128 * j, 128),
                                                  idt(), mw(),
                                                  False, j == 3, waits=[],
                                                  inc=j == 3)
                        for hf in range(2):
                            te = act(lambda e, hf=hf, par=par: e.activation(
                                pb(par, hf), ps[:, 1024 * hf:1024 * hf + 1024],
                                AF.Exp, bias=zcol(), scale=1.0),
                                waits=[lastb[2 * hf], lastb[2 * hf + 1],
                                       probs_read[(par, hf)], t_z])
                            exp_tok[hf] = te
                            t_exp[hf] = te
                        for head in av_order:
                            u = head // 2
                            bk, off = 4 + u // 4, 128 * (u % 4)
                            p0 = 64 * (head % 2)
                            bkey = (u // 4, p0)
                            first = bkey not in av_started
                            av_started[bkey] = True
                            stop = av_last[bkey] == (st, head)
                            r, j = head % 4, head // 4
                            pslice = pb(par, r // 2, 512 * (r % 2) + 128 * j, 128)
                            t_mm = mm(
                                bank(bk, off, 128, p0, 64),
                                vaug(st - sup * spt, 33 * head, 64),
                                pslice,
                                first, stop,
                                waits=[t_exp[r // 2], t_vad, av_clear],
                                tp=(0, p0), skipg=True, inc=head in (13, 15))
                            if t_mm is not None:
                                t_avmm = t_mm
                                probs_read[(par, r // 2)] = t_mm
                    if not acc_init[slot]:
                        t_acc = dve(lambda e, slot=slot: e.tensor_copy(
                            accslot(slot), ps2[:, 0:1024]), waits=[t_avmm])
                        acc_init[slot] = True
                    else:
                        t_acc = dve(lambda e, slot=slot: e.tensor_tensor(
                            accslot(slot), ps2[:, 0:1024], accslot(slot), OP.add),
                            waits=[t_avmm])
                    t_acc_ev[slot] = t_acc
                    av_clear = t_acc

            if stop_after == "D":
                t_a0 = t_acc_ev.get(0)
                tD = dma("sync", dout[0:128, 0:min(OWN, 1024)],
                         accslot(0, 0, min(OWN, 1024)), waits=[t_a0])
                w.waitall_sync()
                for t in w.all_dma("gpsimd"):
                    w.wait("gpsimd", t)
                return
            w.gate(w.all_dma("sync") + w.all_dma("gpsimd"))
            t_pe_end = w.tok("pe")
            t_wo = t_wo_early if early_wo else []
            for u in range(0 if early_wo else 0, 8):
                if early_wo and u < 4:
                    continue
                t_wo.append(dma("sync", wo_u(u), dwop[128 * u:128 * u + 128, :],
                                waits=[t_pe_end, w.tok("dv")]))
            if early_wo:
                wo_tok = t_wo
            else:
                wo_tok = t_wo
            t_rp_prev = None
            rec_prev = [None, None]
            t_nm_s = [None] * nslot
            for slot in range(nslot):
                db = slot % 2
                gtoks = []
                for jj in range(2):
                    srcj = bass.AP(arena, (32 + 64 * jj) * CAP + o_R4 + slot * 1024,
                                   [[CAP, 1], [1, 1024]])
                    dstj = bass.AP(arena, 8 * jj * CAP + o_den + 128 * db,
                                   [[CAP, 8], [1, 128]])
                    gtoks.append(dma("gpsimd", dstj, srcj,
                                     waits=[t_acc_ev[slot], w.tok("dv")] +
                                           ([rec_prev[db]] if rec_prev[db] else [])))
                t_rd = dve(lambda e, db=db: e.reciprocal(rdenS(db), denS(db)),
                           waits=gtoks)
                rec_prev[db] = t_rd
                rep_mms = []
                for u in range(8):
                    rep_mms.append(mm(bank(u // 4, 128 * (u % 4), 128), pnt(u),
                                      rdenS(db), u % 4 == 0, u % 4 == 3,
                                      waits=[t_rd, exp_tok[0], exp_tok[1]] +
                                            ([t_rp_prev] if t_rp_prev else []),
                                      inc=u % 4 == 3))
                t_rp = []
                for u in range(8):
                    t_rp.append(dve(lambda e, u=u: e.tensor_copy(
                        scr(128 * u, 128), bank(u // 4, 128 * (u % 4), 128)),
                        waits=[rep_mms[4 * (u // 4) + 3]]))
                t_nm = dve(lambda e, slot=slot: e.tensor_tensor(
                    accslot(slot), accslot(slot), scr(0, 1024), OP.mult),
                    waits=[t_rp[-1]])
                t_rp_prev = t_nm
                t_nm_s[slot] = t_nm
            flat = [(s, j) for s in range(nslot) for j in range(8)]
            L = len(flat)
            PF = 6
            gp_t = [None] * L
            ring_f = [None] * 8
            st_tok = [None] * 4

            def issue_gp(i):
                s, j = flat[i]
                r = i % 8
                gp_t[i] = dma("gpsimd", partial(128 * r, 128),
                              dout[128 * j:128 * j + 128,
                                   128 * s:128 * s + 128],
                              waits=([ring_f[r]] if ring_f[r] else []))

            for i in range(min(PF, L)):
                issue_gp(i)
            wo_close = [None] * 2
            t_xuse = [None, None]
            for s in range(nslot):
                sb = s % 2
                t_xl = dma("sync", xet(sb)[:], dxo[128 * s:128 * s + 128, :],
                           waits=[t_pe_end] +
                                 ([t_xuse[sb]] if t_xuse[sb] else []))
                for jg in range(2):
                    for j in range(4 * jg, 4 * jg + 4):
                        bk, off = 2 + j // 4, 128 * (j % 4)
                        for u in range(8):
                            mm(bank(bk, off, 128),
                               wo_u(u)[:, 128 * j:128 * j + 128],
                               accslot(s, 128 * u, 128),
                               u == 0 and j % 4 == 0, False,
                               waits=[t_nm_s[s], wo_tok[u], w.tok("dv")],
                               inc=False)
                        t_c = mm(bank(bk, off, 128),
                                 xet(sb)[:, 128 * j:128 * j + 128], idt(),
                                 False, j % 4 == 3, waits=[t_xl],
                                 inc=j % 4 == 3)
                    wo_close[jg] = t_c
                t_xuse[sb] = wo_close[1]
                for j in range(8):
                    i = 8 * s + j
                    if i + PF < L:
                        issue_gp(i + PF)
                    r, ob = i % 8, i % 4
                    bk, off = 2 + j // 4, 128 * (j % 4)
                    t_f = dve(lambda e, ob=ob, bk=bk, off=off, r=r:
                              e.tensor_tensor(outsb(128 * ob, 128),
                                              bank(bk, off, 128),
                                              partial(128 * r, 128), OP.add),
                              waits=[gp_t[i], wo_close[j // 4]] +
                                    ([st_tok[ob]] if st_tok[ob] else []))
                    ring_f[r] = t_f
                    st_tok[ob] = dma("sync", dout[128 * j:128 * j + 128,
                                                  128 * s:128 * s + 128],
                                     outsb(128 * ob, 128), waits=[t_f])
            for t in w.all_dma("sync") + w.all_dma("gpsimd"):
                w.wait("sync", t)
            for t in w.all_dma("gpsimd"):
                w.wait("gpsimd", t)

        @block.sync
        def _(e):
            emit(Walk(nc, "sync", e, sems))

        @block.gpsimd
        def _(e):
            emit(Walk(nc, "gpsimd", e, sems))

        @block.tensor
        def _(e):
            emit(Walk(nc, "tensor", e, sems))

        @block.vector
        def _(e):
            emit(Walk(nc, "vector", e, sems))

        @block.scalar
        def _(e):
            emit(Walk(nc, "scalar", e, sems))

    return nc


def run_cores(inputs, T, cores, stop_after=None):
    nc = build_nc(T, stop_after)
    in_maps, owns = [], []
    for core in cores:
        d, own = _core_inputs(inputs, core, T)
        in_maps.append(d)
        owns.append(own)
    res = run_bass_kernel_spmd(nc, in_maps, core_ids=list(range(len(cores))))
    return res, owns


def kernel(**inputs):
    T = 4096
    res, owns = run_cores(inputs, T, list(range(8)))
    out = np.empty((4, 4096, 1024), dtype=np.float32)
    for ci in range(8):
        b = ci // 2
        o = np.asarray(res.results[ci]["out"])
        out[b, owns[ci], :] = o.T
    return out


# version 16 (final)
# speedup vs baseline: 1.0097x; 1.0097x over previous
import numpy as np
import concourse.bass as bass
import concourse.mybir as mybir
from concourse.bass_utils import run_bass_kernel_spmd

F32 = mybir.dt.float32
OP = mybir.AluOpType
AF = mybir.ActivationFunctionType
AX = mybir.AxisListType.X

H, HD, ROPE, LAT = 16, 64, 32, 128
NEG = -1e30
LN_EPS = 1e-5
RMS_EPS = 1e-6



def _consts():
    c = {}
    c["ident"] = np.eye(128, dtype=np.float32)
    p2 = np.zeros((4, 128, 16), dtype=np.float32)
    for j in range(4):
        for r in range(128):
            p2[j, r, 4 * j + r // 32] = 1.0
    c["p2"] = p2
    c["e0"] = np.ascontiguousarray(p2.transpose(0, 2, 1))
    pn = np.zeros((8, 16, 128), dtype=np.float32)
    for u in range(8):
        for rr in range(128):
            if rr % 64 < 34:
                pn[u, 8 * (rr // 64) + u, rr] = 1.0
    c["patnorm"] = np.ascontiguousarray(pn)
    deni = np.zeros((1, 528), dtype=np.float32)
    for h in range(H):
        deni[0, 33 * h + 32] = 1.0
    c["denind"] = deni
    return c


def _core_inputs(inputs, core, T):
    nslot = T // 256
    b, odd = core // 2, core % 2
    x = np.asarray(inputs["L0_attn_x"], dtype=np.float32)
    Wq = np.asarray(inputs["L0_attn_Wq"], dtype=np.float32)
    Wv = np.asarray(inputs["L0_attn_Wv"], dtype=np.float32)
    Wo = np.asarray(inputs["L0_attn_Wo"], dtype=np.float32)
    d = {}
    xb = np.ascontiguousarray(x[b % x.shape[0]][:T])
    d["xb"] = xb
    own = np.concatenate([np.arange(256 * i + 128 * odd, 256 * i + 128 * odd + 128)
                          for i in range(nslot)])
    d["xown"] = np.ascontiguousarray(xb[own])
    perm = np.concatenate([np.arange(64 * h, 64 * h + 32) for h in range(H)] +
                          [np.arange(64 * h + 32, 64 * h + 64) for h in range(H)])
    d["wq"] = np.ascontiguousarray(Wq[:, perm])
    d["wkv"] = np.ascontiguousarray(np.asarray(inputs["L0_attn_Wkv"], dtype=np.float32))
    d["wk"] = np.ascontiguousarray(np.asarray(inputs["L0_attn_Wk"], dtype=np.float32))
    wva = np.zeros((LAT, 544), dtype=np.float32)
    for h in range(H):
        wva[:, 33 * h:33 * h + 32] = Wv[:, 32 * h:32 * h + 32]
    d["wva"] = wva
    wop = np.zeros((1024, 1024), dtype=np.float32)
    for h in range(H):
        wop[64 * h:64 * h + 32] = Wo[64 * h:64 * h + 32]
    d["wop"] = wop
    d["qg"] = np.asarray(inputs["L0_attn_q_g"], dtype=np.float32).reshape(1, HD).copy()
    d["kg"] = np.asarray(inputs["L0_attn_k_g"], dtype=np.float32).reshape(1, ROPE).copy()
    d["fw"] = np.ascontiguousarray(np.asarray(inputs["L0_ffn_weight"], dtype=np.float32))
    d["fb"] = np.asarray(inputs["L0_ffn_bias"], dtype=np.float32).reshape(1, 1024).copy()
    sr = np.arange(128).reshape(128, 1)
    qr = np.arange(128).reshape(1, 128)
    tri = np.where(sr > qr, np.float32(NEG), np.float32(0)).astype(np.float32)
    allm = np.full((128, 128), np.float32(NEG), dtype=np.float32)
    zero = np.zeros((128, 128), dtype=np.float32)
    d["m0w"] = np.ascontiguousarray(zero if odd else tri)
    d["m1w"] = np.ascontiguousarray(tri if odd else allm)
    NT_, NO_ = T // 128, T // 256
    osel = np.zeros((NT_, NO_), dtype=np.float32)
    for i in range(NO_):
        osel[2 * i + odd, i] = 1.0
    d["ownsel"] = osel
    d.update(_consts())
    return d, own



N_LD, N_GD = 16, 8
DMA_POOL = {"sync": [f"ld{i}" for i in range(N_LD)],
            "gpsimd": [f"gd{i}" for i in range(N_GD)]}
SEM_NAMES = ["pe", "dv", "ac"] + DMA_POOL["sync"] + DMA_POOL["gpsimd"]


class Walk:
    def __init__(self, nc, cur, handle, sems):
        self.nc = nc
        self.cur = cur
        self.h = handle
        self.sems = sems
        self.cnt = {k: 0 for k in sems}
        self.rr = {"sync": 0, "gpsimd": 0}
        self.waited = {k: 0 for k in sems}

    def wait(self, engname, tok):
        if tok is None or self.cur != engname:
            return
        sem, val = tok
        if val > self.waited[sem]:
            self.h.wait_ge(self.sems[sem], val)
            self.waited[sem] = val

    def tok(self, sem):
        return (sem, self.cnt[sem])

    def all_dma(self, q):
        return [(s, self.cnt[s]) for s in DMA_POOL[q] if self.cnt[s] > 0]

    def waitall_sync(self):
        for t in self.all_dma("sync") + self.all_dma("gpsimd"):
            self.wait("sync", t)

    def gate(self, tokens):
        if self.cur in ("tensor", "vector", "scalar", "gpsimd"):
            for t in tokens:
                self.wait(self.cur, t)

    def _emit(self, engname, fn, inc, amt, waits):
        for t in waits:
            self.wait(engname, t)
        if self.cur == engname:
            inst = fn(self.h)
            if inc is not None and inst is not None:
                inst.then_inc(self.sems[inc], amt)
        if inc is not None:
            self.cnt[inc] += amt
            return (inc, self.cnt[inc])
        return None

    def dma(self, q, out, in_, waits=()):
        pool = DMA_POOL[q]
        sem = pool[self.rr[q] % len(pool)]
        self.rr[q] += 1
        prior = self.cnt[sem]
        flow = [(sem, prior)] if prior > 0 else []
        return self._emit(q, lambda e: e.dma_start(out=out, in_=in_), sem, 16,
                          list(waits) + flow)

    def mm(self, out, lhsT, rhs, start, stop, waits=(), tp=None, skipg=False,
           inc=True):
        return self._emit(
            "tensor",
            lambda e: e.matmul(out, lhsT, rhs, start=start, stop=stop,
                               tile_position=tp, skip_group_check=skipg),
            "pe" if inc else None, 1, waits)

    def pet(self, out, in_, ident, waits=()):
        return self._emit("tensor", lambda e: e.transpose(out, in_, ident),
                          "pe", 1, waits)

    def dve(self, fn, waits=()):
        return self._emit("vector", fn, "dv", 1, waits)

    def act(self, fn, waits=()):
        return self._emit("scalar", fn, "ac", 1, waits)



def build_nc(T=4096, stop_after=None):
    nslot = T // 256
    OWN = 128 * nslot
    sup_w = min(1024, T)
    nsup = T // sup_w
    spt = sup_w // 128
    NT = T // 128
    NO = OWN // 128

    nc = bass.Bass("TRN2", target_bir_lowering=False)
    dx = nc.dram_tensor("xb", [T, 1024], F32, kind="ExternalInput").ap()
    dxo = nc.dram_tensor("xown", [OWN, 1024], F32, kind="ExternalInput").ap()
    dwq = nc.dram_tensor("wq", [1024, 1024], F32, kind="ExternalInput").ap()
    dwkv = nc.dram_tensor("wkv", [1024, LAT], F32, kind="ExternalInput").ap()
    dwk = nc.dram_tensor("wk", [LAT, 512], F32, kind="ExternalInput").ap()
    dwva = nc.dram_tensor("wva", [LAT, 544], F32, kind="ExternalInput").ap()
    dwop = nc.dram_tensor("wop", [1024, 1024], F32, kind="ExternalInput").ap()
    dqg = nc.dram_tensor("qg", [1, HD], F32, kind="ExternalInput").ap()
    dkg = nc.dram_tensor("kg", [1, ROPE], F32, kind="ExternalInput").ap()
    dfw = nc.dram_tensor("fw", [1024, 1024], F32, kind="ExternalInput").ap()
    dfb = nc.dram_tensor("fb", [1, 1024], F32, kind="ExternalInput").ap()
    dm0 = nc.dram_tensor("m0w", [128, 128], F32, kind="ExternalInput").ap()
    dm1 = nc.dram_tensor("m1w", [128, 128], F32, kind="ExternalInput").ap()
    dident = nc.dram_tensor("ident", [128, 128], F32, kind="ExternalInput").ap()
    dosel = nc.dram_tensor("ownsel", [NT, NO], F32, kind="ExternalInput").ap()
    dp2 = nc.dram_tensor("p2", [4, 128, 16], F32, kind="ExternalInput").ap()
    de0 = nc.dram_tensor("e0", [4, 16, 128], F32, kind="ExternalInput").ap()
    dpn = nc.dram_tensor("patnorm", [8, 16, 128], F32, kind="ExternalInput").ap()
    ddeni = nc.dram_tensor("denind", [1, 528], F32, kind="ExternalInput").ap()
    dout = nc.dram_tensor("out", [1024, OWN], F32, kind="ExternalOutput").ap()

    from contextlib import ExitStack
    es = ExitStack()
    with es:
        CAP = 51800
        arena = es.enter_context(nc.sbuf_tensor("arena", [128, CAP], F32))
        cur = [0]

        def alloc(n):
            off = cur[0]
            cur[0] += n
            return off

        def tl(off, w, p0=0, p=128):
            return arena[p0:p0 + p, off:off + w]

        def ap3(off, dims, p0=0, np_=128):
            return bass.AP(arena, p0 * CAP + off, [[CAP, np_]] + dims)

        o_kv = alloc(T)
        o_q32 = alloc(4 * OWN)
        o_probs = alloc(2 * 1024)
        o_scr = alloc(1024)
        o_scr2 = alloc(1024)
        o_m0 = alloc(128)
        o_m1 = alloc(128)
        o_pat = alloc(1024)
        o_p2 = alloc(4 * 16)
        o_id = alloc(128)
        o_rows = alloc(4096)
        o_stat = alloc(4 * 32)
        o_sb = alloc(8)
        o_den = alloc(256)
        o_rden = alloc(256)
        o_osel = alloc(NO)
        o_tm = alloc(2 * 1024)
        o_qsc = alloc(3 * 32)
        o_wk = alloc(512)
        o_wva = alloc(544)
        o_wkv = alloc(8 * LAT)
        o_R4 = alloc(16384)
        o_R5 = alloc(8 * 544 + 64)
        o_R6 = alloc(4096)
        o_qs = o_R4 + 12288
        o_qtm = o_R5
        assert cur[0] <= CAP, f"arena {cur[0]} > {CAP}"

        kvf = lambda w_, off=0: tl(o_kv + off, w_, 0, LAT)
        probs = lambda i, off=0, w=1024: tl(o_probs + i * 1024 + off, w)

        def pb(par, hf, off=0, w=1024):
            base = (o_probs + 1024 * hf) if par == 0 else \
                   (o_scr if hf == 0 else o_scr2)
            return tl(base + off, w)
        scr = lambda off=0, w=1024: tl(o_scr + off, w)
        scr2 = lambda off=0, w=1024, p0=0, p=128: tl(o_scr2 + off, w, p0, p)
        m0t = lambda: tl(o_m0, 128)
        m1t = lambda: tl(o_m1, 128)
        e0t = lambda j: tl(o_pat + 128 * j, 128, 32, 16)
        eqt = lambda j: tl(o_pat + 128 * (4 + j), 128, 32, 16)
        pnt = lambda u: tl(o_pat + 128 * u, 128, 0, 16)
        p2t = lambda j: tl(o_p2 + j * 16, 16)
        idt = lambda: tl(o_id, 128)
        rowp = lambda p0, off, w: tl(o_rows + off, w, p0, 1)
        stc = lambda k, c, w=1: tl(o_stat + k * 32 + c, w)
        oselt = lambda: tl(o_osel, NO, 0, NT)
        denS = lambda i: tl(o_den + 128 * i, 128, 0, 16)
        rdenS = lambda i: tl(o_rden + 128 * i, 128, 0, 16)
        early_wo = T >= 4096
        tmt = lambda i: tl(o_tm + i * 1024, 1024)
        qs = lambda i, off=0, w=1024, p0=0, p=128: tl(o_qs + i * 1024 + off, w, p0, p)
        qsc = lambda i: tl(o_qsc + i * 32, 32)
        qtm = lambda: tl(o_qtm, 1024)
        wkt = lambda: tl(o_wk, 512, 0, LAT)
        wvat = lambda: tl(o_wva, 544, 0, LAT)
        wkvt = lambda i: tl(o_wkv + i * LAT, LAT)
        areps = lambda: tl(o_R4 + 8192, 128)
        wqt = lambda i: tl(o_R4 + i * 1024, 1024)
        qtmp = lambda off=0, w=1024: tl(o_R4 + 8192 + 2048 + off, w)
        wdT = lambda i: tl(o_R4 + i * 1024, 1024)
        xdfm2 = lambda p, i, off=0, w=512: tl(
            (o_R6 if p == 0 else o_R4 + 8192) + i * 512 + off, w)
        xet = lambda i: tl(o_probs + 1024 * i, 1024)
        accslot = lambda s, off=0, w=1024: tl(o_R4 + s * 1024 + off, w)
        vaug = lambda tc, off=0, w=544: tl(o_R5 + tc * 544 + off, w)
        xfms = lambda i: tl(o_R6 + i * 128, 128)
        xdfm = lambda i, off=0, w=512: tl(o_R6 + i * 512 + off, w)
        khat = lambda j, off=0, w=None: tl(o_R6 + j * sup_w + off,
                                           sup_w if w is None else w)
        wo_u = lambda u: (tl((o_kv if early_wo else o_R5) + u * 1024, 1024)
                          if u < 4 else tl(o_R6 + (u - 4) * 1024, 1024))
        partial = lambda off=0, w=128: tl(o_tm + off, w)
        outsb = lambda off=0, w=128: tl(o_tm + 1024 + off, w)
        epsln = lambda p0=0, p=128: tl(o_sb + 0, 1, p0, p)
        epsrm = lambda p0=0, p=128: tl(o_sb + 1, 1, p0, p)
        zcol = lambda p0=0, p=128: tl(o_sb + 2, 1, p0, p)
        mrow = lambda: rowp(0, 0, 128)
        arow = lambda: rowp(0, 128, 128)
        csq = lambda off, w: rowp(0, 256 + off, w)
        cskv = lambda: rowp(0, 1280, LAT)
        denir = lambda off=0, w=528: rowp(0, 1408 + off, w)
        ggr = lambda: rowp(0, 1936, 32)
        qgr = lambda: rowp(0, 2000, HD)
        kgr = lambda: rowp(0, 2064, ROPE)
        ggt = lambda: rowp(0, 2128, 128)
        o16 = lambda: rowp(0, 2272, 16)
        fbr = lambda off, w: rowp(0, 2448 + off, w)
        onesr = lambda w=512: rowp(0, 3472, w)

        ps = es.enter_context(nc.psum_tensor("psA", [128, 2048]))
        ps2 = es.enter_context(nc.psum_tensor("psB", [128, 2048]))
        qint = es.enter_context(nc.sbuf_tensor("qint", [128, 1024],
                                               mybir.dt.int32))
        qintf = qint.bitcast(F32)

        def bank(i, off=0, w=512, p0=0, p=128):
            base = ps if i < 4 else ps2
            j = i if i < 4 else i - 4
            return base[p0:p0 + p, 512 * j + off:512 * j + off + w]

        def setap2(r, off=0, w=512, p0=0, p=128):
            return ps[p0:p0 + p, 512 * r + off:512 * r + off + w]

        sems = {k: es.enter_context(nc.semaphore(k)) for k in SEM_NAMES}
        block = es.enter_context(nc.Block())

        def quant(w, o_in, mcol, acol, waits):
            dve = w.dve
            c_am, c_s, c_az = qsc(0), qsc(1), qsc(2)

            if mcol is not None:
                t = dve(lambda e: e.tensor_scalar(qs(0), tl(o_in, 1024), mcol, None,
                                                  OP.subtract), waits=waits)
            else:
                t = dve(lambda e: e.tensor_copy(qs(0), tl(o_in, 1024)), waits=waits)
            xm3 = ap3(o_qs, [[32, 32], [1, 32]])
            t = dve(lambda e: e.tensor_reduce(c_am, xm3, AX, OP.max,
                                              apply_absolute_value=True), waits=[t])
            if acol is not None:
                t = dve(lambda e: e.tensor_scalar(c_am, c_am, acol, None, OP.mult),
                        waits=[t])
            t = dve(lambda e: e.tensor_scalar(c_s, c_am, 1.0 / 7.5, 1e-30,
                                              OP.mult, OP.max), waits=[t])
            t = dve(lambda e: e.reciprocal(c_az, c_s), waits=[t])
            if acol is not None:
                t = dve(lambda e: e.tensor_scalar(c_az, c_az, acol, 8.0,
                                                  OP.mult, OP.mult), waits=[t])
            else:
                t = dve(lambda e: e.tensor_scalar(c_az, c_az, 8.0, None, OP.mult),
                        waits=[t])
            t = dve(lambda e: e.tensor_scalar(c_s, c_s, 0.125, None, OP.mult),
                    waits=[t])
            az3 = ap3(o_qsc + 64, [[1, 32], [0, 32]])
            s83 = ap3(o_qsc + 32, [[1, 32], [0, 32]])
            q13 = ap3(o_qs + 1024, [[32, 32], [1, 32]])
            t = dve(lambda e: e.tensor_tensor(q13, xm3, az3, OP.mult), waits=[t])
            t = dve(lambda e: e.tensor_tensor(qs(2), qs(1), qs(1), OP.mult), waits=[t])
            t = dve(lambda e: e.tensor_scalar(qs(3), qs(2), 256.0, -16.0,
                                              OP.is_ge, OP.mult), waits=[t])
            t = dve(lambda e: e.tensor_scalar(qs(2), qs(2), 1024.0, -8.0,
                                              OP.is_ge, OP.mult), waits=[t])
            t = dve(lambda e: e.scalar_tensor_tensor(qs(3), qs(3), 1.0, qs(2),
                                                     OP.mult, OP.add), waits=[t])
            t = dve(lambda e: e.tensor_scalar(qs(2), qs(3), 0.03125, 1.0,
                                              OP.mult, OP.add), waits=[t])
            t = dve(lambda e: e.tensor_tensor(qs(2), qs(1), qs(2), OP.mult), waits=[t])
            t = dve(lambda e: e.tensor_scalar(qs(2), qs(2), 16.5, None, OP.add),
                    waits=[t])
            t = dve(lambda e: e.tensor_scalar(qintf[:], qs(2), 8388608.0, None,
                                              OP.add), waits=[t])
            t = dve(lambda e: e.tensor_scalar(qs(0), qintf[:], 8388608.0, None,
                                              OP.subtract), waits=[t])
            t = dve(lambda e: e.tensor_tensor(qs(0), qs(2), qs(0), OP.subtract),
                    waits=[t])
            t = dve(lambda e: e.tensor_scalar(qs(2), qs(0), 0.0, None,
                                              OP.is_le), waits=[t])
            t = dve(lambda e: e.scalar_tensor_tensor(qs(2), qs(2), -0.5, qs(0),
                                                     OP.add, OP.add), waits=[t])
            t = dve(lambda e: e.tensor_scalar(qs(0), qs(3), 0.0078125, 0.0625,
                                              OP.mult, OP.add), waits=[t])
            t = dve(lambda e: e.scalar_tensor_tensor(qs(0), qs(0), 1.0, qs(3),
                                                     OP.mult, OP.mult), waits=[t])
            t = dve(lambda e: e.tensor_tensor(qs(0), qs(2), qs(0), OP.mult), waits=[t])
            t = dve(lambda e: e.tensor_tensor(qs(0), qs(0), qs(2), OP.add), waits=[t])
            t = dve(lambda e: e.tensor_tensor(qs(1), qs(1), qs(0), OP.subtract),
                    waits=[t])
            qo3 = ap3(o_qtm, [[32, 32], [1, 32]])
            t = dve(lambda e: e.tensor_tensor(qo3, q13, s83, OP.mult), waits=[t])
            return t

        def stats(w, dsrc, ntiles, tuse):
            dve, dma = w.dve, w.dma
            bufs = [tmt(0), tmt(1), xet(0), xet(1)]
            tuse4 = [tuse[0], tuse[1], tuse[0], tuse[1]]
            last = None
            for it in range(ntiles):
                bi = it % 4
                t_l = dma("sync", bufs[bi][:], dsrc[128 * it:128 * it + 128, :],
                          waits=[tuse4[bi]])
                t1 = dve(lambda e, it=it, bi=bi: e.reduce_sum(
                    stc(0, it), bufs[bi][:], AX), waits=[t_l])
                t2 = dve(lambda e, bi=bi: e.tensor_tensor(qs(3), bufs[bi][:],
                                                          bufs[bi][:], OP.mult),
                         waits=[t1])
                last = dve(lambda e, it=it: e.tensor_reduce(
                    stc(1, it), qs(3), AX, OP.add), waits=[t2])
                tuse4[bi] = t2
            return last

        def stat_finish(w, n, t_in):
            dve, act = w.dve, w.act
            t = dve(lambda e: e.tensor_scalar(stc(2, 0, n), stc(0, 0, n),
                                              1.0 / 1024, None, OP.mult), waits=[t_in])
            t = dve(lambda e: e.tensor_tensor(qs(3, 0, n), stc(2, 0, n),
                                              stc(2, 0, n), OP.mult), waits=[t])
            t = dve(lambda e: e.scalar_tensor_tensor(qs(3, 64, n), stc(1, 0, n),
                                                     1.0 / 1024, qs(3, 0, n),
                                                     OP.mult, OP.subtract), waits=[t])
            t = act(lambda e: e.activation(qs(3, 0, n), qs(3, 64, n), AF.Ln,
                                           bias=epsln(), scale=1.0), waits=[t])
            t = act(lambda e: e.activation(stc(3, 0, n), qs(3, 0, n), AF.Exp,
                                           bias=zcol(), scale=-0.5), waits=[t])
            t = dve(lambda e: e.tensor_scalar(stc(0, 0, n), stc(2, 0, n), -1.0, None,
                                              OP.mult), waits=[t])
            return t

        def emit(w: Walk):
            dve, act, mm, pet, dma = w.dve, w.act, w.mm, w.pet, w.dma

            tld = []
            tld.append(dma("sync", idt(), dident[:]))
            tld.append(dma("sync", m0t(), dm0[:]))
            tld.append(dma("sync", m1t(), dm1[:]))
            for j in range(4):
                tld.append(dma("sync", p2t(j), dp2[j]))
                tld.append(dma("sync", e0t(j), de0[j]))
            for u in range(8):
                tld.append(dma("sync", pnt(u), dpn[u]))
            t_osel = dma("sync", oselt(), dosel[:])
            tld.append(t_osel)
            tld.append(dma("sync", qgr(), dqg[:]))
            tld.append(dma("sync", kgr(), dkg[:]))
            tld.append(dma("sync", fbr(0, 1024), dfb[:]))
            tld.append(dma("sync", denir(), ddeni[:]))
            t_wkv = []
            for i in range(8):
                t_wkv.append(dma("sync", wkvt(i), dwkv[128 * i:128 * i + 128, :]))
            t_wk = dma("sync", wkt(), dwk[:])
            t_wva = dma("sync", wvat(), dwva[:])
            w.gate(tld + t_wkv + [t_wk, t_wva])

            t_eps = dve(lambda e: e.memset(epsln(), LN_EPS))
            dve(lambda e: e.tensor_copy(
                scr2(512, 1).bitcast(mybir.dt.bfloat16)[0:1, 0:2],
                epsln().bitcast(mybir.dt.bfloat16)[0:1, 0:2]),
                waits=[t_eps])
            dve(lambda e: e.memset(epsrm(), RMS_EPS))
            t_z = dve(lambda e: e.memset(zcol(), 0.0))
            t_or = dve(lambda e: e.memset(onesr(512), 1.0))
            t_oc = dve(lambda e: e.memset(scr(0, 1), 1.0))
            t_o16 = dve(lambda e: e.memset(o16(), 1.0))
            t = dve(lambda e: e.tensor_tensor(ggr(), rowp(0, 2000, 32), kgr(),
                                              OP.mult))
            t = dve(lambda e: e.tensor_scalar(ggr(), ggr(), 0.125, None, OP.mult),
                    waits=[t])
            for rr in range(4):
                t = dve(lambda e, rr=rr: e.tensor_copy(rowp(0, 2128 + 32 * rr, 32),
                                                       ggr()), waits=[t])
            tmm = mm(bank(7, 0, 128, 32, 16), o16(), ggt(), True, True,
                     waits=[t, t_o16])
            t_eq = None
            for j in range(4):
                t_eq = dve(lambda e, j=j: e.tensor_tensor(
                    eqt(j), e0t(j), bank(7, 0, 128, 32, 16), OP.mult), waits=[tmm])

            for i in range(8):
                tmm = mm(bank(6, 0, LAT, 0, 1), scr(0, 1), wkvt(i),
                         i == 0, i == 7, waits=[t_oc, t_wkv[i]], inc=i == 7)
            t_cskv = dve(lambda e: e.tensor_copy(cskv(), bank(6, 0, LAT, 0, 1)),
                         waits=[tmm])

            tuse = [None, None]
            t_st = stats(w, dx, NT, tuse)
            t_fin = stat_finish(w, NT, t_st)

            abufs = [tmt(0), tmt(1), xet(0), xet(1)]
            tuse4 = [t_fin] * 4
            t_kv_prev = None
            for it in range(NT):
                bi = it % 4
                t_l = dma("sync", abufs[bi][:], dx[128 * it:128 * it + 128, :],
                          waits=[tuse4[bi]])
                tpm = pet(bank(3, 0, 128, 0, 1), stc(0, it), idt(),
                          waits=[t_fin, w.tok("dv")])
                t_evm = dve(lambda e: e.tensor_copy(mrow(), bank(3, 0, 128, 0, 1)),
                            waits=[tpm])
                tpa = pet(bank(6, 0, 128, 0, 1), stc(3, it), idt(),
                          waits=[t_fin, w.tok("dv")])
                t_eva = dve(lambda e: e.tensor_copy(arow(), bank(6, 0, 128, 0, 1)),
                            waits=[tpa])
                tra = mm(bank(7, 0, 128), onesr(128), arow(), True, True,
                         waits=[t_eva, t_or])
                t_ar = dve(lambda e: e.tensor_copy(areps(), bank(7, 0, 128)),
                           waits=[tra])
                t_ev = []
                for ci in range(8):
                    tp = pet(bank(ci // 4, 128 * (ci % 4), 128),
                             abufs[bi][:, 128 * ci:128 * ci + 128], idt(),
                             waits=[t_l, w.tok("dv")])
                    t_ev.append(dve(lambda e, ci=ci: e.tensor_copy(
                        xfms(ci), bank(ci // 4, 128 * (ci % 4), 128)), waits=[tp]))
                for ci in range(8):
                    tmm = mm(bank(2, 0, 128), wkvt(ci), xfms(ci), ci == 0, False,
                             waits=[t_ev[ci]] + ([t_kv_prev] if ci == 0 else []),
                             inc=False)
                tmm = mm(bank(2, 0, 128), cskv(), mrow(), False, True,
                         waits=[t_evm, t_cskv])
                t_kv = dve(lambda e, it=it: e.tensor_tensor(
                    kvf(128, 128 * it), bank(2, 0, 128),
                    areps()[0:LAT, :], OP.mult), waits=[tmm, t_ar])
                t_kv_prev = t_kv
                tuse4[bi] = t_kv
            t_kv_done = w.tok("dv")

            if stop_after == "A":
                tA = dma("sync", dout[0:LAT, 0:min(OWN, T)], kvf(min(OWN, T), 0),
                         waits=[t_kv_done])
                w.waitall_sync()
                return
            w.gate(w.all_dma("sync") + w.all_dma("gpsimd"))
            t_wq = []
            for i in range(8):
                t_wq.append(dma("sync", wqt(i), dwq[128 * i:128 * i + 128, :],
                                waits=[t_kv_done]))
            tuse = [t_kv_done, t_kv_done]
            tg1 = pet(bank(4, 0, 128, 0, NT), stc(2, 0, NT), idt(),
                      waits=[t_fin, w.tok("dv")])
            tg1e = dve(lambda e: e.tensor_copy(tl(o_scr2, 128, 0, NT),
                                               bank(4, 0, 128, 0, NT)),
                       waits=[tg1])
            tg2 = pet(bank(5, 0, 128, 0, NT), stc(3, 0, NT), idt(),
                      waits=[t_fin, w.tok("dv")])
            tg2e = dve(lambda e: e.tensor_copy(tl(o_scr2 + 128, 128, 0, NT),
                                               bank(5, 0, 128, 0, NT)),
                       waits=[tg2])
            g1 = mm(bank(4, 128, 128, 0, NO), oselt(), tl(o_scr2, 128, 0, NT),
                    True, True, waits=[tg1e, t_osel])
            g2 = mm(bank(5, 128, 128, 0, NO), oselt(),
                    tl(o_scr2 + 128, 128, 0, NT), True, True,
                    waits=[tg2e, t_osel])
            e1 = dve(lambda e: e.tensor_copy(tl(o_scr2, 128, 0, NO),
                                             bank(4, 128, 128, 0, NO)),
                     waits=[g1])
            e2 = dve(lambda e: e.tensor_copy(tl(o_scr2 + 128, 128, 0, NO),
                                             bank(5, 128, 128, 0, NO)),
                     waits=[g2])
            b1 = pet(bank(4, 256, NO), tl(o_scr2, 128, 0, NO),
                     idt()[0:NO, 0:NO], waits=[e1, w.tok("dv")])
            b2 = pet(bank(5, 256, NO), tl(o_scr2 + 128, 128, 0, NO),
                     idt()[0:NO, 0:NO], waits=[e2, w.tok("dv")])
            f1 = dve(lambda e: e.tensor_copy(stc(2, 0, NO), bank(4, 256, NO)),
                     waits=[b1])
            f2 = dve(lambda e: e.tensor_copy(stc(3, 0, NO), bank(5, 256, NO)),
                     waits=[b2])
            t_fin = dve(lambda e: e.tensor_scalar(stc(0, 0, NO), stc(2, 0, NO),
                                                  -1.0, None, OP.mult),
                        waits=[f1, f2])
            t_csq = None
            for ch in range(2):
                for i in range(8):
                    tmm = mm(bank(7, 0, 512, 0, 1), scr(0, 1),
                             wqt(i)[:, 512 * ch:512 * ch + 512],
                             i == 0, i == 7, waits=[t_wq[i], t_oc, w.tok("dv")],
                             inc=i == 7)
                t_csq = dve(lambda e, ch=ch: e.tensor_copy(
                    csq(512 * ch, 512), bank(7, 0, 512, 0, 1)), waits=[tmm])
            t_prev_q = [None] * 8
            t_prev_rep = None
            t_prev_lnq = None
            t_32 = None
            for ot in range(NO):
                bi = ot % 2
                t_l = dma("sync", tmt(bi)[:], dxo[128 * ot:128 * ot + 128, :],
                          waits=[tuse[bi]])
                tpm = pet(bank(6, 384, 128, 0, 1), stc(0, ot), idt(),
                          waits=[t_fin, w.tok("dv")])
                t_evm = dve(lambda e: e.tensor_copy(mrow(), bank(6, 384, 128, 0, 1)),
                            waits=[tpm])
                tpa = pet(bank(7, 384, 128, 0, 1), stc(3, ot), idt(),
                          waits=[t_fin, w.tok("dv")])
                t_eva = dve(lambda e: e.tensor_copy(arow(), bank(7, 384, 128, 0, 1)),
                            waits=[tpa])
                tra = mm(bank(7, 128, 128), onesr(128), arow(), True, True,
                         waits=[t_eva, t_or])
                t_ar = dve(lambda e: e.tensor_copy(areps(), bank(7, 128, 128)),
                           waits=[tra])
                t_ev = []
                for ci in range(8):
                    tp = pet(bank(ci // 4, 128 * (ci % 4), 128),
                             tmt(bi)[:, 128 * ci:128 * ci + 128], idt(),
                             waits=[t_l, w.tok("dv")])
                    t_ev.append(dve(lambda e, ci=ci: e.tensor_copy(
                        xfms(ci), bank(ci // 4, 128 * (ci % 4), 128)), waits=[tp]))
                t_q = []
                for j in range(8):
                    bk, off = 2 + j // 4, 128 * (j % 4)
                    for ci in range(8):
                        tmm = mm(bank(bk, off, 128), wqt(ci)[:, 128 * j:128 * j + 128],
                                 xfms(ci), ci == 0 and j % 4 == 0, False,
                                 waits=[t_ev[ci], t_wq[ci]] +
                                       ([t_prev_q[j]] if ci == 0 else []),
                                 inc=False)
                    tmm = mm(bank(bk, off, 128), csq(128 * j, 128), mrow(),
                             False, j % 4 == 3,
                             waits=[t_evm, t_csq], inc=j % 4 == 3)
                    t_q.append(tmm)
                t_qe = []
                for j in range(8):
                    bk, off = 2 + j // 4, 128 * (j % 4)
                    t_qe.append(dve(lambda e, j=j, bk=bk, off=off: e.tensor_tensor(
                        qtmp(128 * j, 128), bank(bk, off, 128),
                        areps(), OP.mult), waits=[t_q[3 + 4 * (j // 4)], t_ar]))
                    t_prev_q[j] = t_qe[-1]
                t_q2 = dve(lambda e: e.tensor_tensor(scr(0, 1024), qtmp(0, 1024),
                                                     qtmp(0, 1024), OP.mult),
                           waits=[t_qe[-1]])
                for j in range(8):
                    tmm = mm(bank(6, 0, 128, 32, 16), p2t(j % 4),
                             scr(128 * j, 128), j == 0, j == 7,
                             waits=[t_q2] + ([t_prev_lnq] if j == 0 and t_prev_lnq
                                             else []), inc=j == 7)
                t_r = act(lambda e: e.activation(scr2(0, 128, 32, 16),
                                                 bank(6, 0, 128, 32, 16), AF.Ln,
                                                 bias=epsrm(32, 16), scale=1.0 / HD),
                          waits=[tmm, t_eps])
                t_r = act(lambda e: e.activation(scr2(128, 128, 32, 16),
                                                 scr2(0, 128, 32, 16), AF.Exp,
                                                 bias=zcol(32, 16), scale=-0.5),
                          waits=[t_r, t_z])
                t_prev_lnq = t_r
                for j in range(4):
                    tmm = mm(bank(7, 0, 128), eqt(j), scr2(128, 128, 32, 16),
                             True, True,
                             waits=[t_r, t_eq] + ([t_prev_rep] if t_prev_rep else []))
                    t_rp = dve(lambda e: e.tensor_copy(scr2(256, 128), bank(7, 0, 128)),
                               waits=[tmm])
                    t_32 = dve(lambda e, j=j, ot=ot: e.tensor_tensor(
                        tl(o_q32 + j * OWN + 128 * ot, 128), qtmp(128 * j, 128),
                        scr2(256, 128), OP.mult), waits=[t_rp])
                    t_prev_rep = t_32
                tuse[bi] = t_32
            t_q32_done = w.tok("dv")

            if stop_after == "B":
                for j in range(4):
                    dma("sync", dout[128 * j:128 * j + 128, 0:OWN],
                        tl(o_q32 + j * OWN, OWN), waits=[t_q32_done])
                w.waitall_sync()
                return
            w.gate(w.all_dma("sync") + w.all_dma("gpsimd"))
            tuse = [t_q32_done, t_q32_done]
            t_wdT = None
            for wt in range(8):
                bi = wt % 2
                t_l = dma("sync", tmt(bi)[:], dfw[128 * wt:128 * wt + 128, :],
                          waits=[tuse[bi]] + ([t_wq[wt]] if True else []))
                t_qz = quant(w, o_tm + bi * 1024, None, None, [t_l])
                for ci in range(8):
                    tp = pet(bank(ci // 4, 128 * (ci % 4), 128),
                             qtm()[:, 128 * ci:128 * ci + 128], idt(),
                             waits=[t_qz, w.tok("dv")])
                    t_wdT = dve(lambda e, ci=ci, wt=wt: e.tensor_copy(
                        wdT(ci)[:, 128 * wt:128 * wt + 128],
                        bank(ci // 4, 128 * (ci % 4), 128)), waits=[tp])
                tuse[bi] = t_qz
            if stop_after == "C0":
                for ci in range(4):
                    dma("sync", dout[128 * ci:128 * ci + 128, 0:min(OWN, 1024)],
                        wdT(ci)[:, 0:min(OWN, 1024)], waits=[w.tok("dv")])
                w.waitall_sync()
                return
            t_pw = []
            t_prev_f = [None] * 4
            n_oc = (OWN + 511) // 512
            t_sub_oc = [None] * n_oc
            bias_mm = {}

            def ffn_half(oc_p, jlist):
                po = oc_p % 2
                cw_p = min(512, OWN - 512 * oc_p)
                for j in jlist:
                    bk = 4 + j % 4
                    for ci in range(8):
                        mm(bank(bk, 0, cw_p),
                           wdT(ci)[:, 128 * j:128 * j + 128],
                           xdfm2(po, ci, 0, cw_p), ci == 0, False,
                           waits=([t_sub_oc[oc_p], t_wdT] if ci == 0
                                  else []) +
                                 ([t_prev_f[j % 4]] if ci == 0 and
                                  t_prev_f[j % 4] else []), inc=False)
                    bias_mm[(oc_p, j)] = mm(
                        bank(bk, 0, cw_p), fbr(128 * j, 128), onesr(cw_p),
                        False, True, waits=[t_or])

            def tf_half(oc_p, jlist):
                cw_p = min(512, OWN - 512 * oc_p)
                for j in jlist:
                    bk = 4 + j % 4
                    hb = 512 * (j % 2)
                    t_f = dve(lambda e, j=j, bk=bk, hb=hb, cw_p=cw_p:
                              e.tensor_copy(scr2(hb, cw_p), bank(bk, 0, cw_p)),
                              waits=[bias_mm[(oc_p, j)]] +
                                    ([t_pw[-2]] if len(t_pw) >= 2 else []))
                    t_prev_f[j % 4] = t_f
                    t_pw.append(dma("sync",
                                    dout[128 * j:128 * j + 128,
                                         512 * oc_p:512 * oc_p + cw_p],
                                    scr2(hb, cw_p), waits=[t_f]))

            hold = {"t_sub": None}

            def qtile(ot, par):
                bi = ot % 2
                t_l = dma("sync", tmt(bi)[:], dxo[128 * ot:128 * ot + 128, :],
                          waits=[tuse[bi]])
                t_qz = quant(w, o_tm + bi * 1024,
                             None if stop_after == 'Cq2' else stc(2, ot),
                             None if stop_after == 'Cq2' else stc(3, ot),
                             [t_l])
                if stop_after in ("Cq", "Cq2") and ot == 0:
                    dma("sync", dout[0:128, 0:min(OWN, 1024)],
                        qtm()[:, 0:min(OWN, 1024)], waits=[t_qz])
                    w.waitall_sync()
                    return True
                sub = ot % 4
                t_sub = hold["t_sub"]
                for ci in range(8):
                    tp = pet(bank(ci // 4, 128 * (ci % 4), 128),
                             qtm()[:, 128 * ci:128 * ci + 128], idt(),
                             waits=[t_qz, w.tok("dv")])
                    t_sub = dve(lambda e, ci=ci, sub=sub, par=par:
                                e.tensor_copy(
                        xdfm2(par, ci, 128 * sub, 128),
                        bank(ci // 4, 128 * (ci % 4), 128)), waits=[tp])
                if stop_after == "Cx" and ot == 0:
                    dma("sync", dout[0:128, 0:min(OWN, 512)],
                        xdfm2(0, 0, 0, min(OWN, 512)), waits=[w.tok("dv")])
                    w.waitall_sync()
                    return True
                tuse[bi] = t_qz
                hold["t_sub"] = t_sub
                return False

            for oc in range(n_oc + 1):
                subs = (list(range(min(512, OWN - 512 * oc) // 128))
                        if oc < n_oc else [])
                hh = (len(subs) + 1) // 2
                par = oc % 2
                if oc >= 1:
                    ffn_half(oc - 1, [0, 1, 2, 3])
                for sub in subs[:hh]:
                    if qtile(4 * oc + sub, par):
                        return
                if oc >= 1:
                    tf_half(oc - 1, [0, 1, 2, 3])
                    ffn_half(oc - 1, [4, 5, 6, 7])
                for sub in subs[hh:]:
                    if qtile(4 * oc + sub, par):
                        return
                if oc < n_oc:
                    t_sub_oc[oc] = hold["t_sub"]
                    if stop_after == "C1" and oc == 0:
                        cw1 = min(OWN, 512)
                        for ci in range(8):
                            dma("sync", dout[128 * ci:128 * ci + 128, 0:cw1],
                                xdfm2(0, ci, 0, cw1), waits=[w.tok("dv")])
                        w.waitall_sync()
                        return
                if oc >= 1:
                    tf_half(oc - 1, [4, 5, 6, 7])
            t_pw_done = None

            if stop_after == "C":
                w.waitall_sync()
                for t in w.all_dma("gpsimd"):
                    w.wait("gpsimd", t)
                return
            w.gate(w.all_dma("sync") + w.all_dma("gpsimd"))
            t_vp = dve(lambda e: e.memset(tl(o_R5 + spt * 544, 64), 0.0))
            for tc in range(spt):
                t_vp = dve(lambda e, tc=tc: e.memset(vaug(tc, 528, 16), 0.0),
                           waits=[t_vp])
            d0 = w.tok("dv")
            exp_tok = [d0, d0]
            av_clear = d0
            probs_read = {(p_, h_): d0 for p_ in range(2) for h_ in range(2)}
            pcnt = [0]
            acc_init = [False] * nslot
            t_acc_ev = {}
            t_khd = None
            t_vad = None
            t_prev_lnk = None

            for sup in range(nsup):
                nch = (sup_w + 511) // 512
                t_e_list = []
                for j in range(4):
                    for ch in range(nch):
                        cw = min(512, sup_w - 512 * ch)
                        tmm = mm(bank(ch, 0, cw), wkt()[:, 128 * j:128 * j + 128],
                                 kvf(cw, sup * sup_w + 512 * ch), True, True,
                                 waits=[t_kv_done, exp_tok[0], exp_tok[1],
                                        av_clear, w.tok("dv")], inc=ch == nch - 1)
                    t_e = dve(lambda e, j=j: e.tensor_copy(
                        khat(j), ps[:, 0:sup_w]), waits=[tmm])
                    t_e_list.append(t_e)
                    t_k2 = dve(lambda e, j=j: e.tensor_tensor(
                        scr(0, sup_w), khat(j), khat(j), OP.mult),
                        waits=[t_e] + ([probs_read[(1, 0)]] if j == 0 else []))
                    for ch in range(nch):
                        cw = min(512, sup_w - 512 * ch)
                        tmm = mm(bank(2 + ch, 0, cw, 32, 16), p2t(j),
                                 scr(512 * ch, cw), j == 0, j == 3,
                                 waits=[t_k2] + ([t_prev_lnk, exp_tok[1]]
                                                 if j == 0 else []),
                                 inc=j == 3 and ch == nch - 1)
                t_r = act(lambda e: e.activation(scr2(0, sup_w, 32, 16),
                                                 ps[32:48, 1024:1024 + sup_w], AF.Ln,
                                                 bias=epsrm(32, 16), scale=1.0 / ROPE),
                          waits=[tmm, t_eps, probs_read[(1, 1)]])
                t_r = act(lambda e: e.activation(probs(0, 0, sup_w)[32:48, :],
                                                 scr2(0, sup_w, 32, 16), AF.Exp,
                                                 bias=zcol(32, 16), scale=-0.5),
                          waits=[t_r, t_z, probs_read[(0, 0)]])
                t_prev_lnk = t_r
                t_kh = None
                for j in range(4):
                    for ch in range(nch):
                        cw = min(512, sup_w - 512 * ch)
                        tmm = mm(bank(ch, 0, cw), e0t(j),
                                 probs(0, 512 * ch, cw)[32:48, :], True, True,
                                 waits=[t_r, w.tok("dv")], inc=ch == nch - 1)
                    probs_read[(0, 0)] = tmm
                    t_rp = dve(lambda e: e.tensor_copy(scr(0, sup_w), ps[:, 0:sup_w]),
                               waits=[tmm])
                    t_kh = dve(lambda e, j=j: e.tensor_tensor(
                        khat(j), khat(j), scr(0, sup_w), OP.mult), waits=[t_rp])
                t_khd = t_kh
                if stop_after == "D0" and sup == 0:
                    dma("sync", dout[0:128, 0:min(OWN, sup_w)],
                        khat(0, 0, min(OWN, sup_w)), waits=[t_khd])
                    w.waitall_sync()
                    for t in w.all_dma("gpsimd"):
                        w.wait("gpsimd", t)
                    return
                t_v = None
                for tc in range(spt):
                    base = sup * sup_w + 128 * tc
                    for (c0, cw, bk) in [(0, 512, 0), (512, 16, 1)]:
                        tmm = mm(bank(bk, 0, cw), kvf(128, base),
                                 wvat()[:, c0:c0 + cw], True, False,
                                 waits=[t_khd, t_wva, w.tok("dv")], inc=False)
                        tmm = mm(bank(bk, 0, cw), onesr(128),
                                 denir(c0, cw), False, True, waits=[t_or])
                        t_v = dve(lambda e, tc=tc, c0=c0, cw=cw, bk=bk: e.tensor_copy(
                            vaug(tc, c0, cw), bank(bk, 0, cw)), waits=[tmm])
                t_vad = t_v
                if early_wo and sup == nsup - 1:
                    t_kv_free = w.tok("pe")
                    t_wo_early = []
                    for u in range(4):
                        t_wo_early.append(dma(
                            "sync", wo_u(u), dwop[128 * u:128 * u + 128, :],
                            waits=[t_kv_free]))
                if stop_after == "D1" and sup == 0:
                    dma("sync", dout[0:128, 0:min(OWN, 544)],
                        vaug(0, 0, min(OWN, 544)), waits=[t_vad])
                    w.waitall_sync()
                    for t in w.all_dma("gpsimd"):
                        w.wait("gpsimd", t)
                    return

                for slot in range(nslot):
                    lo = sup * spt
                    hi = min(2 * slot + 2, sup * spt + spt)
                    if lo >= hi:
                        continue
                    av_started = {}
                    av_last = {}
                    av_order = ([h for h in range(16) if h % 4 < 2] +
                                [h for h in range(16) if h % 4 >= 2])
                    for st in range(lo, hi):
                        for head in av_order:
                            av_last[((head // 2) // 4, 64 * (head % 2))] = (st, head)
                    t_avmm = None
                    for st in range(lo, hi):
                        masked = st in (2 * slot, 2 * slot + 1)
                        par = pcnt[0] % 2
                        pcnt[0] += 1
                        t_exp = [None, None]
                        lastb = [None] * 4
                        for r in range(4):
                          for j in range(4):
                            h = 4 * j + r
                            lastb[r] = mm(
                                setap2(r, 128 * j, 128),
                                khat(h // 4, 128 * (st - sup * spt), 128)[32 * r:32 * r + 32, :],
                                tl(o_q32 + (h // 4) * OWN + 128 * slot, 128)[32 * r:32 * r + 32, :],
                                j == 0, j == 3 and not masked,
                                waits=[t_vad, t_q32_done, exp_tok[r // 2]],
                                tp=(32 * r, 0), inc=j == 3 and not masked)
                        if masked:
                            mw = m0t if st == 2 * slot else m1t
                            for r in range(4):
                                for j in range(4):
                                    lastb[r] = mm(setap2(r, 128 * j, 128),
                                                  idt(), mw(),
                                                  False, j == 3, waits=[],
                                                  inc=j == 3)
                        for hf in range(2):
                            te = act(lambda e, hf=hf, par=par: e.activation(
                                pb(par, hf), ps[:, 1024 * hf:1024 * hf + 1024],
                                AF.Exp, bias=zcol(), scale=1.0),
                                waits=[lastb[2 * hf], lastb[2 * hf + 1],
                                       probs_read[(par, hf)], t_z])
                            exp_tok[hf] = te
                            t_exp[hf] = te
                        for head in av_order:
                            u = head // 2
                            bk, off = 4 + u // 4, 128 * (u % 4)
                            p0 = 64 * (head % 2)
                            bkey = (u // 4, p0)
                            first = bkey not in av_started
                            av_started[bkey] = True
                            stop = av_last[bkey] == (st, head)
                            r, j = head % 4, head // 4
                            pslice = pb(par, r // 2, 512 * (r % 2) + 128 * j, 128)
                            t_mm = mm(
                                bank(bk, off, 128, p0, 64),
                                vaug(st - sup * spt, 33 * head, 64),
                                pslice,
                                first, stop,
                                waits=[t_exp[r // 2], t_vad, av_clear],
                                tp=(0, p0), skipg=True, inc=head in (13, 15))
                            if t_mm is not None:
                                t_avmm = t_mm
                                probs_read[(par, r // 2)] = t_mm
                    if not acc_init[slot]:
                        t_acc = dve(lambda e, slot=slot: e.tensor_copy(
                            accslot(slot), ps2[:, 0:1024]), waits=[t_avmm])
                        acc_init[slot] = True
                    else:
                        t_acc = dve(lambda e, slot=slot: e.tensor_tensor(
                            accslot(slot), ps2[:, 0:1024], accslot(slot), OP.add),
                            waits=[t_avmm])
                    t_acc_ev[slot] = t_acc
                    av_clear = t_acc

            if stop_after == "D":
                t_a0 = t_acc_ev.get(0)
                tD = dma("sync", dout[0:128, 0:min(OWN, 1024)],
                         accslot(0, 0, min(OWN, 1024)), waits=[t_a0])
                w.waitall_sync()
                for t in w.all_dma("gpsimd"):
                    w.wait("gpsimd", t)
                return
            w.gate(w.all_dma("sync") + w.all_dma("gpsimd"))
            t_pe_end = w.tok("pe")
            t_wo = t_wo_early if early_wo else []
            for u in range(0 if early_wo else 0, 8):
                if early_wo and u < 4:
                    continue
                t_wo.append(dma("sync", wo_u(u), dwop[128 * u:128 * u + 128, :],
                                waits=[t_pe_end, w.tok("dv")]))
            if early_wo:
                wo_tok = t_wo
            else:
                wo_tok = t_wo
            t_rp_prev = None
            rec_prev = [None, None]
            t_nm_s = [None] * nslot
            for slot in range(nslot):
                db = slot % 2
                gtoks = []
                for jj in range(2):
                    srcj = bass.AP(arena, (32 + 64 * jj) * CAP + o_R4 + slot * 1024,
                                   [[CAP, 1], [1, 1024]])
                    dstj = bass.AP(arena, 8 * jj * CAP + o_den + 128 * db,
                                   [[CAP, 8], [1, 128]])
                    gtoks.append(dma("gpsimd", dstj, srcj,
                                     waits=[t_acc_ev[slot], w.tok("dv")] +
                                           ([rec_prev[db]] if rec_prev[db] else [])))
                t_rd = dve(lambda e, db=db: e.reciprocal(rdenS(db), denS(db)),
                           waits=gtoks)
                rec_prev[db] = t_rd
                rep_mms = []
                for u in range(8):
                    rep_mms.append(mm(bank(u // 4, 128 * (u % 4), 128), pnt(u),
                                      rdenS(db), u % 4 == 0, u % 4 == 3,
                                      waits=[t_rd, exp_tok[0], exp_tok[1]] +
                                            ([t_rp_prev] if t_rp_prev else []),
                                      inc=u % 4 == 3))
                t_rp = []
                for u in range(8):
                    t_rp.append(dve(lambda e, u=u: e.tensor_copy(
                        scr(128 * u, 128), bank(u // 4, 128 * (u % 4), 128)),
                        waits=[rep_mms[4 * (u // 4) + 3]]))
                t_nm = dve(lambda e, slot=slot: e.tensor_tensor(
                    accslot(slot), accslot(slot), scr(0, 1024), OP.mult),
                    waits=[t_rp[-1]])
                t_rp_prev = t_nm
                t_nm_s[slot] = t_nm
            flat = [(s, j) for s in range(nslot) for j in range(8)]
            L = len(flat)
            PF = 6
            gp_t = [None] * L
            ring_f = [None] * 8
            st_tok = [None] * 4

            def issue_gp(i):
                s, j = flat[i]
                r = i % 8
                gp_t[i] = dma("gpsimd", partial(128 * r, 128),
                              dout[128 * j:128 * j + 128,
                                   128 * s:128 * s + 128],
                              waits=([ring_f[r]] if ring_f[r] else []))

            for i in range(min(PF, L)):
                issue_gp(i)
            wo_close = [None] * 2
            t_xuse = [None, None]
            for s in range(nslot):
                sb = s % 2
                t_xl = dma("sync", xet(sb)[:], dxo[128 * s:128 * s + 128, :],
                           waits=[t_pe_end] +
                                 ([t_xuse[sb]] if t_xuse[sb] else []))
                for jg in range(2):
                    for j in range(4 * jg, 4 * jg + 4):
                        bk, off = 2 + j // 4, 128 * (j % 4)
                        for u in range(8):
                            mm(bank(bk, off, 128),
                               wo_u(u)[:, 128 * j:128 * j + 128],
                               accslot(s, 128 * u, 128),
                               u == 0 and j % 4 == 0, False,
                               waits=[t_nm_s[s], wo_tok[u], w.tok("dv")],
                               inc=False)
                        t_c = mm(bank(bk, off, 128),
                                 xet(sb)[:, 128 * j:128 * j + 128], idt(),
                                 False, j % 4 == 3, waits=[t_xl],
                                 inc=j % 4 == 3)
                    wo_close[jg] = t_c
                t_xuse[sb] = wo_close[1]
                for j in range(8):
                    i = 8 * s + j
                    if i + PF < L:
                        issue_gp(i + PF)
                    r, ob = i % 8, i % 4
                    bk, off = 2 + j // 4, 128 * (j % 4)
                    t_f = dve(lambda e, ob=ob, bk=bk, off=off, r=r:
                              e.tensor_tensor(outsb(128 * ob, 128),
                                              bank(bk, off, 128),
                                              partial(128 * r, 128), OP.add),
                              waits=[gp_t[i], wo_close[j // 4]] +
                                    ([st_tok[ob]] if st_tok[ob] else []))
                    ring_f[r] = t_f
                    st_tok[ob] = dma("sync", dout[128 * j:128 * j + 128,
                                                  128 * s:128 * s + 128],
                                     outsb(128 * ob, 128), waits=[t_f])
            for t in w.all_dma("sync") + w.all_dma("gpsimd"):
                w.wait("sync", t)
            for t in w.all_dma("gpsimd"):
                w.wait("gpsimd", t)

        @block.sync
        def _(e):
            emit(Walk(nc, "sync", e, sems))

        @block.gpsimd
        def _(e):
            emit(Walk(nc, "gpsimd", e, sems))

        @block.tensor
        def _(e):
            emit(Walk(nc, "tensor", e, sems))

        @block.vector
        def _(e):
            emit(Walk(nc, "vector", e, sems))

        @block.scalar
        def _(e):
            emit(Walk(nc, "scalar", e, sems))

    return nc


def run_cores(inputs, T, cores, stop_after=None):
    nc = build_nc(T, stop_after)
    in_maps, owns = [], []
    for core in cores:
        d, own = _core_inputs(inputs, core, T)
        in_maps.append(d)
        owns.append(own)
    res = run_bass_kernel_spmd(nc, in_maps, core_ids=list(range(len(cores))))
    return res, owns


def kernel(**inputs):
    T = 4096
    res, owns = run_cores(inputs, T, list(range(8)))
    out = np.empty((4, 4096, 1024), dtype=np.float32)
    for ci in range(8):
        b = ci // 2
        o = np.asarray(res.results[ci]["out"])
        out[b, owns[ci], :] = o.T
    return out
